# version 27 (final)
# baseline (speedup 1.0000x reference)
import contextlib
import numpy as np
import concourse.bass as bass
from concourse.bass import IndirectOffsetOnAxis
import concourse.mybir as mybir
from concourse.bass_utils import run_bass_kernel_spmd

F32 = mybir.dt.float32
BF16 = mybir.dt.bfloat16
I32 = mybir.dt.int32
AF = mybir.ActivationFunctionType
ALU = mybir.AluOpType
AX = mybir.AxisListType

ENGINES = ("sync", "tensor", "vector", "scalar", "gpsimd")
SERIAL = ("vector", "scalar", "gpsimd")


class _Op:
    __slots__ = ("engine", "fn", "deps", "dma", "group", "needs_inc", "sem_val",
                 "tag", "idx")

    def __init__(self, engine, fn, deps, dma, group, tag):
        self.engine = engine
        self.fn = fn
        self.deps = deps
        self.dma = dma
        self.group = group
        self.needs_inc = False
        self.sem_val = None
        self.tag = tag
        self.idx = None


class Prog:

    def __init__(self, nc):
        self.nc = nc
        self.ops = []
        self.per_engine = {e: [] for e in ENGINES}
        self.groups = {}
        self._last_compute = {e: None for e in ENGINES}

    def op(self, engine, fn, deps=(), dma=False, group=None, tag=""):
        if dma and group is None:
            group = f"q_{engine}"
        deps = [d for d in deps if d is not None]
        if engine in SERIAL and not dma and self._last_compute[engine] is not None:
            deps.append(self._last_compute[engine])
        o = _Op(engine, fn, deps, dma, group if dma else None, tag)
        o.idx = len(self.per_engine[engine])
        self.per_engine[engine].append(o)
        self.ops.append(o)
        if dma:
            self.groups.setdefault(o.group, []).append(o)
        elif engine in SERIAL:
            self._last_compute[engine] = o
        return o

    def emit(self, stack):
        nc = self.nc
        for o in self.ops:
            if o.dma:
                o.needs_inc = True
            for d in o.deps:
                d.needs_inc = True
        counters = {e: 0 for e in ENGINES}
        gcounters = {g: 0 for g in self.groups}
        for e in ENGINES:
            for o in self.per_engine[e]:
                if not o.needs_inc:
                    continue
                if o.dma:
                    gcounters[o.group] += 16
                    o.sem_val = gcounters[o.group]
                else:
                    counters[e] += 1
                    o.sem_val = counters[e]
        sems = {e: stack.enter_context(nc.semaphore(f"tl_{e}")) for e in ENGINES}
        gsems = {g: stack.enter_context(nc.semaphore(f"g_{g}")) for g in self.groups}

        def sem_of(o):
            return gsems[o.group] if o.dma else sems[o.engine]

        def key_of(o):
            return ("g", o.group) if o.dma else ("e", o.engine)

        with nc.Block() as blk:
            for e in ENGINES:
                ops_e = self.per_engine[e]
                if not ops_e:
                    continue

                def make_body(e=e, ops_e=ops_e):
                    def body(eng):
                        watermark = {}
                        for o in ops_e:
                            needed = {}
                            for d in o.deps:
                                if d.engine == o.engine and not d.dma and d.idx >= o.idx:
                                    raise RuntimeError(f"bad dep {d.tag}->{o.tag}")
                                if d.sem_val > watermark.get(key_of(d), 0):
                                    cur = needed.get(key_of(d), (None, 0))[1]
                                    needed[key_of(d)] = (sem_of(d), max(cur, d.sem_val))
                            for k, (s, v) in sorted(needed.items()):
                                eng.wait_ge(s, v)
                                watermark[k] = v
                            inst = o.fn(eng)
                            if o.needs_inc:
                                inst.then_inc(sem_of(o), 16 if o.dma else 1)
                    return body

                getattr(blk, e)(make_body())


class Arena:
    def __init__(self, tensor):
        self.t = tensor
        self.off = 0
        self.peak = 0

    def alloc(self, n):
        off = (self.off + 15) // 16 * 16
        self.off = off + n
        self.peak = max(self.peak, self.off)
        return self.t[:, off:off + n]

    def mark(self):
        return self.off

    def reset(self, off):
        self.off = off


def build(cfg):
    T = cfg["T"]; H = cfg["H"]; E = cfg["E"]
    D = 1024; DH = 64
    DI = H * DH
    NDC = D // 128
    NDI = DI // 128
    NT = T // 128
    BLK = min(512, T)
    TPB = BLK // 128
    NB = T // BLK
    TW = BLK
    WKV = min(512, DI)
    NKV = DI // WKV
    F = 1024
    NFC = F // 128

    nc = bass.Bass("TRN2", target_bir_lowering=False)
    def din(name, shape):
        return nc.dram_tensor(name, shape, F32, kind="ExternalInput").ap()
    x_d = din("x", [T, D])
    wq_d = din("wq", [128, NDC, DI])
    wk_d = din("wk", [128, NDC, DI])
    wv_d = din("wv", [128, NDC, DI])
    wout_d = din("wout", [128, NDI, D])
    wbeta_d = din("wbeta", [128, NDC, H])
    bbeta_d = din("bbeta", [H])
    wgt_d = din("wgt", [128, NDC, E])
    bg_d = din("bg", [E])
    wgate_d = din("wgate", [E, 128, NDC, F])
    wup_d = din("wup", [E, 128, NDC, F])
    wd_d = din("wd", [E, 128, NFC, D])
    bgug_d = din("bgug", [E, 128, NFC])
    bguu_d = din("bguu", [E, 128, NFC])
    bd_d = din("bd", [E, D])
    out_d = nc.dram_tensor("out", [T, D], F32, kind="ExternalOutput").ap()
    CAP0 = cfg.get("cap", 512)
    h2d_d = nc.dram_tensor("h2d", [T, D], BF16, kind="Internal").ap()
    idl_d = nc.dram_tensor("idl", [E * CAP0], I32, kind="Internal").ap()
    oed_d = nc.dram_tensor("oed", [E * CAP0, D], BF16, kind="Internal").ap()
    dbg = cfg.get("dbg", False)
    if dbg:
        dbg_x2 = nc.dram_tensor("dbg_x2", [128, NT * 1024], F32, kind="ExternalOutput").ap()
        dbg_cw = nc.dram_tensor("dbg_cw", [128, NT * E], F32, kind="ExternalOutput").ap()
        dbg_beta = nc.dram_tensor("dbg_beta", [128, NT * H], F32, kind="ExternalOutput").ap()
        dbg_o = nc.dram_tensor("dbg_o", [NT, 128, DI], F32, kind="ExternalOutput").ap()
        dbg_q = nc.dram_tensor("dbg_q", [128, NDI * 128], F32, kind="ExternalOutput").ap()
        dbg_kt = nc.dram_tensor("dbg_kt", [128, DI], F32, kind="ExternalOutput").ap()
        dbg_u = nc.dram_tensor("dbg_u", [128, 64], F32, kind="ExternalOutput").ap()
        dbg_idl = nc.dram_tensor("dbg_idl", [E * cfg.get("cap", 512)], I32, kind="ExternalOutput").ap()
        dbg_dst = nc.dram_tensor("dbg_dst", [128, 2 * NT], I32, kind="ExternalOutput").ap()
        dbg_oed = nc.dram_tensor("dbg_oed", [E * cfg.get("cap", 512), 1024], F32, kind="ExternalOutput").ap()

    mode = cfg.get("mode", "full")
    st = contextlib.ExitStack()
    with st:
        a32 = Arena(st.enter_context(nc.sbuf_tensor("arena32", [128, 19200], F32)))
        a16 = Arena(st.enter_context(nc.sbuf_tensor("arena16", [128, 67760], BF16)))
        psum = [st.enter_context(nc.psum_tensor(f"pb{i}", [128, 512], F32))
                for i in range(8)]
        P = Prog(nc)
        op = P.op

        x_sb = a32.alloc(NT * 1024)
        beta_tok = a32.alloc(NT * H)
        nbeta_tok = a32.alloc(NT * H)
        cw_sb = a32.alloc(NT * E)
        maskL = a32.alloc(128)
        maskUn = a32.alloc(128)
        maskUi = a32.alloc(128)
        iota_f = a32.alloc(128)
        iota_p = a32.alloc(1)
        colscr = a32.alloc(24)
        rn_sb = a32.alloc(2 * max(H, 16))
        lgt_sb = a32.alloc(E)
        eps_col = a32.alloc(1)
        bgug_col = [a32.alloc(NFC) for _ in range(2)]
        s32_sb = a32.alloc(512)
        bguu_col = [a32.alloc(NFC) for _ in range(2)]
        id_bf = a16.alloc(128)
        ones1 = a16.alloc(128)
        bbeta_row = a16.alloc(16)
        bg_row = a16.alloc(16)
        wgt_sb = a16.alloc(NDC * E)

        mark16 = a16.mark()
        sq_scr = a16.alloc(1024)
        sq_scr2 = a16.alloc(1024)
        wq_sb = a16.alloc(NDC * DI)
        wk_sb = a16.alloc(NDC * DI)
        wv_sb = a16.alloc(NDC * DI)
        wout_sb = a16.alloc(NDI * D)
        wbeta_sb = a16.alloc(NDC * H)
        h_stage = [a16.alloc(1024)]
        hf_blk = a16.alloc(NDC * BLK)
        qf_blk = a16.alloc(NDI * BLK)
        ktok_blk = a16.alloc(TPB * DI)
        kf_blk = a16.alloc(NDI * BLK)
        vtok_blk = a16.alloc(TPB * DI)
        o_chunk = a16.alloc(DI)
        of_chunk = a16.alloc(NDI * 128)
        m_pair = [a16.alloc(256) for _ in range(2)]
        mt_pair = [a16.alloc(256) for _ in range(2)]
        gm2 = [[a16.alloc(256) for _ in range(2)] for _ in range(2)]
        r_pair = [a16.alloc(128) for _ in range(2)]
        d1_pair = [a16.alloc(128) for _ in range(2)]
        curbuf = [[a16.alloc(512) for _ in range(2)] for _ in range(2)]
        upair = [[a16.alloc(128) for _ in range(2)] for _ in range(2)]
        oi2q = [a16.alloc(256) for _ in range(2)]
        sf_bf = a16.alloc(512)

        c_iota_f = op("gpsimd", lambda e: e.iota(iota_f, [[1, 128]], channel_multiplier=0,
                      allow_small_or_imprecise_dtypes=True), tag="iota_f")
        c_iota_p = op("gpsimd", lambda e: e.iota(iota_p, [[0, 1]], channel_multiplier=1,
                      allow_small_or_imprecise_dtypes=True), tag="iota_p")
        c_maskL = op("vector", lambda e: e.tensor_scalar(out=maskL, in0=iota_f,
                     scalar1=iota_p[:, 0:1], scalar2=None, op0=ALU.is_lt),
                     deps=[c_iota_f, c_iota_p], tag="maskL")
        c_maskUn = op("vector", lambda e: e.tensor_scalar(out=maskUn, in0=iota_f,
                      scalar1=iota_p[:, 0:1], scalar2=-1.0, op0=ALU.is_gt, op1=ALU.mult),
                      deps=[c_iota_f, c_iota_p], tag="maskUn")
        c_maskUi = op("vector", lambda e: e.tensor_scalar(out=maskUi, in0=iota_f,
                      scalar1=iota_p[:, 0:1], scalar2=None, op0=ALU.is_ge),
                      deps=[c_iota_f, c_iota_p], tag="maskUi")
        c_id = op("vector", lambda e: e.tensor_scalar(out=id_bf, in0=iota_f,
                  scalar1=iota_p[:, 0:1], scalar2=None, op0=ALU.is_equal),
                  deps=[c_iota_f, c_iota_p], tag="id_bf")
        c_ones = op("gpsimd", lambda e: e.memset(ones1[0:1, :], 1.0), tag="ones1")
        c_eps = op("gpsimd", lambda e: e.memset(eps_col, 1e-5), tag="eps")
        c_s0 = op("gpsimd", lambda e: e.memset(s32_sb, 0.0), tag="s0")

        ldw = []
        for name, dst, src in (("wq", wq_sb, wq_d), ("wk", wk_sb, wk_d),
                               ("wv", wv_sb, wv_d), ("wout", wout_sb, wout_d),
                               ("wbeta", wbeta_sb, wbeta_d), ("wgt", wgt_sb, wgt_d)):
            n2 = src.shape[2]
            ldw.append(op("gpsimd", lambda e, dst=dst, src=src, n2=n2: e.dma_start(
                out=dst.rearrange("p (c f) -> p c f", f=n2), in_=src),
                dma=True, group="wconst", tag=f"ld_{name}"))
        ldw.append(op("gpsimd", lambda e: e.dma_start(
            out=bbeta_row[0:1, 0:H], in_=bbeta_d.rearrange("(a b) -> a b", a=1)),
            dma=True, group="wconst", tag="ld_bbr"))
        ldw.append(op("gpsimd", lambda e: e.dma_start(
            out=bg_row[0:1, 0:E], in_=bg_d.rearrange("(a b) -> a b", a=1)),
            dma=True, group="wconst", tag="ld_bgr"))
        wconst_last = ldw[-1]
        xr = x_d.rearrange("(n p) d -> p n d", p=128)
        ld_x = []
        for b in range(NB):
            ld_x.append(op("sync", lambda e, b=b: e.dma_start(
                out=x_sb[:, b * TPB * 1024:(b + 1) * TPB * 1024].rearrange(
                    "p (n d) -> p n d", d=1024),
                in_=xr[:, b * TPB:(b + 1) * TPB, :]),
                dma=True, group="ldx", tag=f"ld_x{b}"))
        ldx_last = ld_x[-1]

        sqread = [None, None]
        ln_prev = {"t5": None, "t9": None}

        def layernorm(src_slice, dst_slice, deps, tagp, dump=None):
            if dump is None:
                dump = sq_scr
            ssum = colscr[:, 0:1]; ssq = colscr[:, 1:2]; mean = colscr[:, 2:3]
            vpe = colscr[:, 3:4]; rstd = colscr[:, 4:5]; mrstd = colscr[:, 5:6]
            t1 = op("vector", lambda e: e.tensor_reduce(
                out=ssum, in_=src_slice, axis=AX.X, op=ALU.add), deps=deps, tag=f"{tagp}_sum")
            t2 = op("scalar", lambda e: e.activation(
                out=dump, in_=src_slice, func=AF.Square, accum_out=ssq),
                deps=deps + [sqread[0], ln_prev["t5"]], tag=f"{tagp}_sq")
            t3 = op("vector", lambda e: e.tensor_scalar(
                out=mean, in0=ssum, scalar1=1.0 / 1024, scalar2=None, op0=ALU.mult),
                deps=[t1], tag=f"{tagp}_mean")
            t4 = op("vector", lambda e: e.tensor_scalar(
                out=vpe, in0=mean, scalar1=mean, scalar2=None, op0=ALU.mult),
                deps=[t3], tag=f"{tagp}_m2")
            t5 = op("vector", lambda e: e.scalar_tensor_tensor(
                out=vpe, in0=ssq, scalar=1.0 / 1024, in1=vpe,
                op0=ALU.mult, op1=ALU.subtract), deps=[t2, t4], tag=f"{tagp}_var")
            t6 = op("scalar", lambda e: e.activation(
                out=rstd, in_=vpe, func=AF.Ln, bias=eps_col),
                deps=[t5, ln_prev["t9"], c_eps], tag=f"{tagp}_ln")
            t7 = op("scalar", lambda e: e.activation(
                out=rstd, in_=rstd, func=AF.Exp, scale=-0.5), deps=[t6], tag=f"{tagp}_exp")
            t8 = op("vector", lambda e: e.tensor_scalar(
                out=mrstd, in0=mean, scalar1=rstd, scalar2=None, op0=ALU.mult),
                deps=[t7, t3], tag=f"{tagp}_mr")
            t9 = op("vector", lambda e: e.tensor_scalar(
                out=dst_slice, in0=src_slice, scalar1=rstd, scalar2=mrstd,
                op0=ALU.mult, op1=ALU.subtract), deps=[t8, t7] + deps, tag=f"{tagp}_nrm")
            ln_prev["t5"] = t5
            ln_prev["t9"] = t9
            return t9

        hf_t = hf_blk.rearrange("p (c w) -> p c w", w=BLK)
        qf_t = qf_blk.rearrange("p (c w) -> p c w", w=BLK)
        kf_t = kf_blk.rearrange("p (c w) -> p c w", w=BLK)
        wq_t = wq_sb.rearrange("p (c f) -> p c f", f=DI)
        wk_t = wk_sb.rearrange("p (c f) -> p c f", f=DI)
        wv_t = wv_sb.rearrange("p (c f) -> p c f", f=DI)
        wout_t = wout_sb.rearrange("p (c f) -> p c f", f=D)
        wbeta_t = wbeta_sb.rearrange("p (c f) -> p c f", f=H)
        wgt_t = wgt_sb.rearrange("p (c f) -> p c f", f=E)

        bankR = [[] for _ in range(8)]

        kt_last = [None]
        dbg_pending = []
        prev_hT = []
        last_dS = None
        last_dS_pair = []
        last_ox = {0: [], 1: []}
        last_oev = {0: [], 1: []}
        prev_s_adds = [c_s0, c_s0]
        res_toks = {}
        prev_scan_join = []
        prev_of_readers = []
        prev_o_readers = []

        for b in range(NB):
            blk_dep = [ldx_last, wconst_last]
            prev_scan = list(prev_scan_join)
            hf_ready = []
            for ti in range(TPB):
                tile = b * TPB + ti
                hst = h_stage[0]
                war = hf_ready[-1:] if ti >= 1 else list(prev_hT)
                ln = layernorm(x_sb[:, tile * 1024:(tile + 1) * 1024], hst,
                               blk_dep + war + prev_scan, f"ln1_{tile}")
                p4bf = psum[4].ap().bitcast(BF16)
                trs_h = []
                for dc in range(NDC):
                    trs_h.append(op("tensor", lambda e, dc=dc, hst=hst, p4bf=p4bf: e.matmul(
                        p4bf[:, dc * 128:(dc + 1) * 128], hst[:, dc * 128:(dc + 1) * 128],
                        id_bf, is_transpose=True, start=(dc == 0), stop=(dc == NDC - 1)),
                        deps=([ln, c_id] + prev_scan + bankR[4]) if dc == 0 else [],
                        tag=f"hT{tile}_{dc}"))
                hf_ready.append(op("vector", lambda e, ti=ti, p4bf=p4bf: e.tensor_copy(
                    hf_t[:, :, ti * 128:(ti + 1) * 128], p4bf[:, 0:NDC * 128].rearrange(
                        "p (c w) -> p c w", w=128)),
                    deps=[trs_h[-1]], tag=f"hTe{tile}"))
                bankR[4] = [hf_ready[-1]]
            hf_last = hf_ready[-1]
            prev_hT = hf_ready[-1:]
            q_evacs = []
            for m in range(NDI):
                bi = 2 + (m % 2)
                bank = psum[bi]
                mm0 = None
                for dc in range(NDC):
                    mm = op("tensor", lambda e, m=m, dc=dc, bank=bank: e.matmul(
                        bank[:, 0:TW], wq_t[:, dc, m * 128:(m + 1) * 128],
                        hf_t[:, dc, 0:TW], start=(dc == 0), stop=(dc == NDC - 1)),
                        deps=([hf_last] + bankR[bi] if dc == 0 else []),
                        tag=f"q{b}_{m}_{dc}")
                q_evacs.append(op("scalar", lambda e, m=m, bank=bank: e.activation(
                    out=qf_t[:, m, 0:TW], in_=bank[:, 0:TW], func=AF.Copy),
                    deps=[mm], tag=f"qe{b}_{m}"))
                bankR[bi] = [q_evacs[-1]]
            kv_done = []
            dep0 = [hf_last, q_evacs[-1]] + prev_scan
            first_b = True
            for ti in range(TPB):
                for dc in range(NDC):
                    op("tensor", lambda e, dc=dc, ti=ti, st_=first_b: e.matmul(
                        psum[2][:, ti * H:(ti + 1) * H],
                        hf_t[:, dc, ti * 128:(ti + 1) * 128],
                        wbeta_t[:, dc, :], start=st_, stop=False),
                        deps=(dep0 + bankR[2]) if first_b else [], tag=f"bt{b}_{ti}_{dc}")
                    first_b = False
            mmb = None
            for ti in range(TPB):
                mmb = op("tensor", lambda e, ti=ti: e.matmul(
                    psum[2][:, ti * H:(ti + 1) * H], ones1[0:1, :], bbeta_row[0:1, 0:H],
                    start=False, stop=(ti == TPB - 1)), deps=[c_ones], tag=f"btb{b}_{ti}")
            bsig = op("scalar", lambda e, b=b: e.activation(
                out=beta_tok[:, b * TPB * H:(b + 1) * TPB * H], in_=psum[2][:, 0:TPB * H],
                func=AF.Sigmoid), deps=[mmb], tag=f"bsig{b}")
            bankR[2] = [bsig]
            bneg = op("vector", lambda e, b=b: e.tensor_scalar(
                out=nbeta_tok[:, b * TPB * H:(b + 1) * TPB * H],
                in0=beta_tok[:, b * TPB * H:(b + 1) * TPB * H],
                scalar1=-1.0, scalar2=None, op0=ALU.mult), deps=[bsig], tag=f"bneg{b}")
            bsigs = [bneg]
            KB = {0: [3, 5], 1: [6, 7]}
            vbanks = [0, 1][:NKV]
            sq2 = [sq_scr, sq_scr2]
            mmk_by_ti = {}

            def kv_mms(ti):
                tile = b * TPB + ti
                kbanks = KB[ti % 2][:NKV]
                mmk_last = mmv_last = None
                extra = list(prev_s_adds) if ti % 2 == 1 else []
                for dc in range(NDC):
                    for nh in range(NKV):
                        mmk_last = op("tensor", lambda e, dc=dc, nh=nh, ti=ti, kb=kbanks[nh]: e.matmul(
                            psum[kb][:, 0:WKV],
                            hf_t[:, dc, ti * 128:(ti + 1) * 128],
                            wk_t[:, dc, nh * WKV:(nh + 1) * WKV],
                            start=(dc == 0), stop=(dc == NDC - 1)),
                            deps=(dep0 + bankR[kbanks[nh]] + extra) if dc == 0 else [],
                            tag=f"k{tile}_{dc}_{nh}")
                        mmv_last = op("tensor", lambda e, dc=dc, nh=nh, ti=ti: e.matmul(
                            psum[vbanks[nh]][:, 0:WKV],
                            hf_t[:, dc, ti * 128:(ti + 1) * 128],
                            wv_t[:, dc, nh * WKV:(nh + 1) * WKV],
                            start=(dc == 0), stop=(dc == NDC - 1)),
                            deps=(dep0 + bankR[vbanks[nh]]) if dc == 0 else [],
                            tag=f"v{tile}_{dc}_{nh}")
                mmk_by_ti[ti] = mmk_last
                vevs = []
                for nh in range(NKV):
                    vevs.append(op("vector", lambda e, nh=nh, ti=ti: e.tensor_copy(
                        vtok_blk[:, ti * DI + nh * WKV: ti * DI + (nh + 1) * WKV],
                        psum[vbanks[nh]][:, 0:WKV]), deps=[mmv_last], tag=f"vev{tile}_{nh}"))
                    bankR[vbanks[nh]] = [vevs[-1]]
                kv_done.extend(vevs)

            def knorm_chain(ti):
                tile = b * TPB + ti
                par = ti % 2
                kbanks = KB[par][:NKV]
                ngrp = WKV // 64
                scr = sq2[par]
                rb = rn_sb[:, par * H:(par + 1) * H]
                rdl = None
                for nh in range(NKV):
                    sq = op("scalar", lambda e, nh=nh, par=par, scr=scr: e.activation(
                        out=scr[:, 0:WKV], in_=psum[kbanks[nh]][:, 0:WKV], func=AF.Square),
                        deps=[mmk_by_ti[ti], sqread[par]], tag=f"ksq{tile}_{nh}")
                    rdl = op("vector", lambda e, nh=nh, ngrp=ngrp, scr=scr, rb=rb: e.tensor_reduce(
                        out=rb[:, nh * ngrp:(nh + 1) * ngrp],
                        in_=scr[:, 0:WKV].rearrange("p (g f) -> p g f", f=64),
                        axis=AX.X, op=ALU.add), deps=[sq], tag=f"krd{tile}_{nh}")
                    sqread[par] = rdl
                    yield
                r1 = op("vector", lambda e, rb=rb: e.tensor_scalar(
                    out=rb[:, 0:H], in0=rb[:, 0:H], scalar1=1e-24, scalar2=None,
                    op0=ALU.max), deps=[rdl], tag=f"kmx{tile}")
                yield
                r2 = op("scalar", lambda e, rb=rb: e.activation(
                    out=rb[:, 0:H], in_=rb[:, 0:H], func=AF.Ln), deps=[r1], tag=f"kln{tile}")
                yield
                r3 = op("scalar", lambda e, rb=rb: e.activation(
                    out=rb[:, 0:H], in_=rb[:, 0:H], func=AF.Exp, scale=-0.5),
                    deps=[r2], tag=f"kexp{tile}")
                yield
                kevs = []
                for h in range(H):
                    nh = (h * 64) // WKV
                    bi = kbanks[nh]
                    off = (h * 64) % WKV
                    if nh == 0:
                        kevs.append(op("vector", lambda e, h=h, bi=bi, off=off, ti=ti, rb=rb: e.tensor_scalar(
                            out=ktok_blk[:, ti * DI + h * 64: ti * DI + (h + 1) * 64],
                            in0=psum[bi][:, off:off + 64], scalar1=rb[:, h:h + 1],
                            scalar2=None, op0=ALU.mult), deps=[r3], tag=f"kev{tile}_{h}"))
                    else:
                        kevs.append(op("scalar", lambda e, h=h, bi=bi, off=off, ti=ti, rb=rb: e.activation(
                            out=ktok_blk[:, ti * DI + h * 64: ti * DI + (h + 1) * 64],
                            in_=psum[bi][:, off:off + 64], func=AF.Identity,
                            scale=rb[:, h:h + 1]), deps=[r3], tag=f"kev{tile}_{h}"))
                    if h % 2 == 1:
                        yield
                for nh in range(NKV):
                    bankR[kbanks[nh]] = [kevs[-1]]
                pT = 4 if par == 0 else 2
                p4bf = psum[pT].ap().bitcast(BF16)
                trs_k = []
                for dc in range(NDI):
                    trs_k.append(op("tensor", lambda e, dc=dc, ti=ti, p4bf=p4bf: e.matmul(
                        p4bf[:, dc * 128:(dc + 1) * 128],
                        ktok_blk[:, ti * DI + dc * 128: ti * DI + (dc + 1) * 128],
                        id_bf, is_transpose=True, start=(dc == 0), stop=(dc == NDI - 1)),
                        deps=([kevs[2 * dc], kevs[2 * dc + 1], c_id] + prev_scan + bankR[pT])
                        if dc == 0 else [kevs[2 * dc], kevs[2 * dc + 1]],
                        tag=f"kT{tile}_{dc}"))
                yield
                kt_last[0] = op("vector", lambda e, ti=ti, p4bf=p4bf: e.tensor_copy(
                    kf_t[:, :, ti * 128:(ti + 1) * 128], p4bf[:, 0:NDI * 128].rearrange(
                        "p (c w) -> p c w", w=128)),
                    deps=[trs_k[-1]], tag=f"kTe{tile}")
                bankR[pT] = [kt_last[0]]
                kv_done.append(kt_last[0])

            for half in range(TPB // 2):
                kv_mms(2 * half)
                kv_mms(2 * half + 1)
                gens = [knorm_chain(2 * half), knorm_chain(2 * half + 1)]
                while gens:
                    for g_ in list(gens):
                        try:
                            next(g_)
                        except StopIteration:
                            gens.remove(g_)

            kv_last = kv_done[-1]

            BAP = {0: 2, 1: 5}
            BSQ1 = {0: 3, 1: 0}
            BSQ2 = {0: 4, 1: 1}
            BOX = {0: 4, 1: 1}
            SQE1 = {0: "vector", 1: "scalar"}
            SQE2 = {0: "scalar", 1: "vector"}
            NPAIR = H // 2
            for cb in range(TPB):
                c = b * TPB + cb
                if c > 0:
                    s_evac = op("vector", lambda e: e.tensor_copy(
                        sf_bf[:, 0:(H // 2) * 64], s32_sb[:, 0:(H // 2) * 64]),
                        deps=prev_s_adds, tag=f"scast{c}")
                else:
                    s_evac = None
                quad_prev = list(last_dS_pair)
                dS_ops = []
                NQ = (NPAIR + 1) // 2
                holder = {"l0": [], "ue0": []}

                def make_pre(q):
                    pairs_q = [2 * q + j for j in range(2) if 2 * q + j < NPAIR]
                    st_pre = {"ctx": {}, "partial": {}}

                    def early():
                        for par, p in enumerate(pairs_q):
                            h0, h1 = 2 * p, 2 * p + 1
                            ccol = cb * 128
                            kf_pair = kf_t[:, p, ccol:ccol + 128]
                            qf_pair = qf_t[:, p, ccol:ccol + 128]
                            v_pair = vtok_blk[:, cb * DI + h0 * 64: cb * DI + h0 * 64 + 128]
                            k_pair = ktok_blk[:, cb * DI + h0 * 64: cb * DI + h0 * 64 + 128]
                            sf_p = [sf_bf[0:64, p * 64:(p + 1) * 64],
                                    sf_bf[64:128, p * 64:(p + 1) * 64]]
                            nb = [nbeta_tok[:, c * H + h0: c * H + h0 + 1],
                                  nbeta_tok[:, c * H + h1: c * H + h1 + 1]]
                            bc = [beta_tok[:, c * H + h0: c * H + h0 + 1],
                                  beta_tok[:, c * H + h1: c * H + h1 + 1]]
                            hstart = [kv_last, kt_last[0], q_evacs[-1], bsigs[-1]]
                            b_done = [None, None]
                            for hh in range(2):
                                rows = slice(64 * hh, 64 * hh + 64)
                                bk = psum[hh]
                                op("tensor", lambda e, kf_pair=kf_pair, rows=rows, bk=bk: e.matmul(
                                    bk[:, 0:128], kf_pair[rows, :], kf_pair[rows, :],
                                    start=True, stop=False),
                                    deps=hstart + bankR[hh], tag=f"A{c}_{p}{hh}")
                                op("tensor", lambda e, kf_pair=kf_pair, qf_pair=qf_pair,
                                   rows=rows, bk=bk, c=c: e.matmul(
                                    bk[:, 128:256], kf_pair[rows, :], qf_pair[rows, :],
                                    start=False, stop=(c == 0)), deps=[], tag=f"G{c}_{p}{hh}")
                                if c > 0:
                                    op("tensor", lambda e, kf_pair=kf_pair, rows=rows, bk=bk,
                                       sf0=sf_p[hh]: e.matmul(
                                        bk[:, 256:320], kf_pair[rows, :], sf0,
                                        start=False, stop=False), deps=[s_evac], tag=f"P{c}_{p}{hh}")
                                    b_done[hh] = op("tensor", lambda e, qf_pair=qf_pair, rows=rows,
                                                    bk=bk, sf0=sf_p[hh]: e.matmul(
                                        bk[:, 320:384], qf_pair[rows, :], sf0,
                                        start=False, stop=True), deps=[], tag=f"Oi{c}_{p}{hh}")
                                else:
                                    b_done[hh] = P.per_engine["tensor"][-1]
                            evs = {}
                            for hh in range(2):
                                bk = psum[hh]
                                evs[("M", hh)] = op("vector", lambda e, bk=bk, hh=hh, nb=nb, par=par: e.scalar_tensor_tensor(
                                    out=m_pair[par][:, hh * 128:(hh + 1) * 128], in0=bk[:, 0:128],
                                    scalar=nb[hh], in1=maskL, op0=ALU.mult, op1=ALU.mult),
                                    deps=[b_done[hh], c_maskL] + holder["l0"], tag=f"M{c}_{p}{hh}")
                                if c > 0:
                                    evs[("d", hh)] = op("vector", lambda e, bk=bk, hh=hh, v_pair=v_pair, par=par: e.tensor_tensor(
                                        out=d1_pair[par][:, hh * 64:(hh + 1) * 64],
                                        in0=v_pair[:, hh * 64:(hh + 1) * 64], in1=bk[:, 256:320],
                                        op=ALU.subtract), deps=[b_done[hh]], tag=f"d{c}_{p}{hh}")
                                    evs[("R", hh)] = op("vector", lambda e, hh=hh, bc=bc, par=par: e.tensor_scalar(
                                        out=r_pair[par][:, hh * 64:(hh + 1) * 64],
                                        in0=d1_pair[par][:, hh * 64:(hh + 1) * 64], scalar1=bc[hh],
                                        scalar2=None, op0=ALU.mult),
                                        deps=[evs[("d", hh)]] + holder["ue0"],
                                        tag=f"R{c}_{p}{hh}")
                                else:
                                    evs[("R", hh)] = op("vector", lambda e, hh=hh, v_pair=v_pair, bc=bc, par=par: e.tensor_scalar(
                                        out=r_pair[par][:, hh * 64:(hh + 1) * 64],
                                        in0=v_pair[:, hh * 64:(hh + 1) * 64], scalar1=bc[hh],
                                        scalar2=None, op0=ALU.mult),
                                        deps=hstart + holder["ue0"], tag=f"R{c}_{p}{hh}")
                            qp = q % 2
                            for hh in range(2):
                                bk = psum[hh]
                                evs[("G", hh)] = op("vector", lambda e, bk=bk, hh=hh, par=par, qp=qp: e.tensor_tensor(
                                    out=gm2[qp][par][:, hh * 128:(hh + 1) * 128], in0=bk[:, 128:256],
                                    in1=maskUi, op=ALU.mult),
                                    deps=[b_done[hh], c_maskUi] + last_ox[qp], tag=f"Gm{c}_{p}{hh}")
                                if c > 0:
                                    evs[("O", hh)] = op("vector", lambda e, bk=bk, hh=hh, par=par, qp=qp: e.tensor_copy(
                                        oi2q[qp][:, par * 128 + hh * 64: par * 128 + (hh + 1) * 64],
                                        bk[:, 320:384]),
                                        deps=[b_done[hh]] + last_oev[qp], tag=f"Oe{c}_{p}{hh}")
                                bankR[hh] = [evs[kk] for kk in evs if kk[1] == hh and kk[0] in "MGd"] \
                                    + ([evs[("O", hh)]] if c > 0 else [])
                            st_pre["partial"][par] = dict(
                                p=p, h0=h0, b_done=b_done, evs=evs, v_pair=v_pair,
                                k_pair=k_pair)

                    def late():
                        for par in sorted(st_pre["partial"]):
                            Y = st_pre["partial"][par]
                            p, b_done, evs = Y["p"], Y["b_done"], Y["evs"]
                            pbf = psum[BAP[par]].ap().bitcast(BF16)
                            tr0 = op("tensor", lambda e, pbf=pbf, par=par: e.matmul(
                                pbf[:, 0:128], m_pair[par][:, 0:128], id_bf,
                                is_transpose=True, start=True, stop=False),
                                deps=[evs[("M", 0)], c_id] + bankR[BAP[par]], tag=f"tr{c}_{p}0")
                            tr1 = op("tensor", lambda e, pbf=pbf, par=par: e.matmul(
                                pbf[:, 128:256], m_pair[par][:, 128:256], id_bf,
                                is_transpose=True, start=False, stop=True),
                                deps=[evs[("M", 1)]], tag=f"tr{c}_{p}1")
                            mMt = op("vector", lambda e, pbf=pbf, par=par: e.tensor_copy(
                                mt_pair[par], pbf[:, 0:256]), deps=[tr0, tr1], tag=f"Mt_{c}_{p}")
                            bankR[BAP[par]] = [mMt]
                            st_pre["ctx"][par] = dict(
                                p=p, h0=Y["h0"], evs=evs, mMt=mMt, k_pair=Y["k_pair"],
                                cur_sl=[m_pair[par][:, 0:128], m_pair[par][:, 128:256]],
                                curT_sl=[mt_pair[par][:, 0:128], mt_pair[par][:, 128:256]],
                                cur_tok=[evs[("M", 0)], evs[("M", 1)], mMt],
                                u_ap=r_pair[par], u_tok=[evs[("R", 0)], evs[("R", 1)]],
                                oi_ok=[evs[("O", 0)], evs[("O", 1)]] if c > 0 else [],
                            )
                    return early, late, st_pre

                pend = None
                for q in range(NQ):
                    if pend is None:
                        e_, l_, st_pre = make_pre(q)
                        e_()
                    else:
                        e_, l_, st_pre = pend
                        pend = None
                    l_()
                    ctx = st_pre["ctx"]
                    nxt_pre = make_pre(q + 1) if q + 1 < NQ else None
                    nlev = cfg.get("nlev", 7)
                    nosq = cfg.get("nosq", False)
                    ue0_list = []
                    for k in range(nlev):
                        for par in ctx:
                            X = ctx[par]
                            uo = upair[par][k % 2]
                            op("tensor", lambda e, ct=X["curT_sl"][0], u_ap=X["u_ap"], bi=BAP[par]: e.matmul(
                                psum[bi][:, 256:320], ct, u_ap[:, 0:64],
                                start=True, stop=False),
                                deps=X["cur_tok"] + X["u_tok"] + bankR[BAP[par]],
                                tag=f"ap{c}_{X['p']}_{k}0")
                            if par == 0:
                                ap1 = op("tensor", lambda e, ct=X["curT_sl"][1], u_ap=X["u_ap"], bi=BAP[par]: e.matmul(
                                    psum[bi][:, 320:384], ct, u_ap[:, 64:128],
                                    start=False, stop=True), deps=[], tag=f"ap{c}_{X['p']}_{k}1")
                                ue = op("vector", lambda e, uo=uo, u_ap=X["u_ap"], bi=BAP[par]: e.scalar_tensor_tensor(
                                    out=uo, in0=psum[bi][:, 256:384], scalar=1.0, in1=u_ap,
                                    op0=ALU.mult, op1=ALU.add), deps=[ap1] + X["u_tok"],
                                    tag=f"ue{c}_{X['p']}_{k}")
                            else:
                                op("tensor", lambda e, ct=X["curT_sl"][1], u_ap=X["u_ap"], bi=BAP[par]: e.matmul(
                                    psum[bi][:, 320:384], ct, u_ap[:, 64:128],
                                    start=False, stop=False), deps=[], tag=f"ap{c}_{X['p']}_{k}1")
                                op("tensor", lambda e, u_ap=X["u_ap"], bi=BAP[par]: e.matmul(
                                    psum[bi][:, 256:320], id_bf, u_ap[:, 0:64],
                                    start=False, stop=False), deps=[c_id], tag=f"ai{c}_{X['p']}_{k}0")
                                ap1 = op("tensor", lambda e, u_ap=X["u_ap"], bi=BAP[par]: e.matmul(
                                    psum[bi][:, 320:384], id_bf, u_ap[:, 64:128],
                                    start=False, stop=True), deps=[], tag=f"ai{c}_{X['p']}_{k}1")
                                ue = op("scalar", lambda e, uo=uo, bi=BAP[par]: e.activation(
                                    out=uo, in_=psum[bi][:, 256:384], func=AF.Copy),
                                    deps=[ap1] + X["u_tok"], tag=f"ue{c}_{X['p']}_{k}")
                            bankR[BAP[par]] = [ue]
                            X["u_ap"], X["u_tok"] = uo, [ue]
                            if k == 0:
                                ue0_list.append(ue)
                            if k < nlev - 1 and not nosq:
                                nxt = curbuf[par][(k + 1) % 2]
                                do_sq1 = k < nlev - 2
                                b1, b2 = BSQ1[par], BSQ2[par]
                                sq_evs = []
                                if do_sq1:
                                    op("tensor", lambda e, ct=X["curT_sl"][0], cu=X["cur_sl"][0], b1=b1: e.matmul(
                                        psum[b1][:, 0:128], ct, cu, start=True, stop=False),
                                        deps=X["cur_tok"] + bankR[b1], tag=f"sqa{c}_{X['p']}_{k}0")
                                    sq1_last = op("tensor", lambda e, ct=X["curT_sl"][1], cu=X["cur_sl"][1], b1=b1: e.matmul(
                                        psum[b1][:, 128:256], ct, cu, start=False, stop=True),
                                        deps=[], tag=f"sqa{c}_{X['p']}_{k}1")
                                op("tensor", lambda e, ct=X["curT_sl"][0], cu=X["cur_sl"][0], b2=b2: e.matmul(
                                    psum[b2][:, 256:384], cu, ct, start=True, stop=False),
                                    deps=(X["cur_tok"] + bankR[b2]), tag=f"sqb{c}_{X['p']}_{k}0")
                                sq2_last = op("tensor", lambda e, ct=X["curT_sl"][1], cu=X["cur_sl"][1], b2=b2: e.matmul(
                                    psum[b2][:, 384:512], cu, ct, start=False, stop=True),
                                    deps=[], tag=f"sqb{c}_{X['p']}_{k}1")
                                if do_sq1:
                                    if SQE1[par] == "scalar":
                                        sq_ev1 = op("scalar", lambda e, nxt=nxt, b1=b1: e.activation(
                                            out=nxt[:, 0:256], in_=psum[b1][:, 0:256], func=AF.Copy),
                                            deps=[sq1_last], tag=f"sqe{c}_{X['p']}_{k}a")
                                    else:
                                        sq_ev1 = op("vector", lambda e, nxt=nxt, b1=b1: e.tensor_copy(
                                            nxt[:, 0:256], psum[b1][:, 0:256]),
                                            deps=[sq1_last], tag=f"sqe{c}_{X['p']}_{k}a")
                                    bankR[b1] = [sq_ev1]
                                    sq_evs.append(sq_ev1)
                                    X["cur_sl"] = [nxt[:, 0:128], nxt[:, 128:256]]
                                if SQE2[par] == "scalar":
                                    sq_ev2 = op("scalar", lambda e, nxt=nxt, b2=b2: e.activation(
                                        out=nxt[:, 256:512], in_=psum[b2][:, 256:512], func=AF.Copy),
                                        deps=[sq2_last], tag=f"sqe{c}_{X['p']}_{k}b")
                                else:
                                    sq_ev2 = op("vector", lambda e, nxt=nxt, b2=b2: e.tensor_copy(
                                        nxt[:, 256:512], psum[b2][:, 256:512]),
                                        deps=[sq2_last], tag=f"sqe{c}_{X['p']}_{k}b")
                                bankR[b2] = [sq_ev2]
                                sq_evs.append(sq_ev2)
                                X["curT_sl"] = [nxt[:, 256:384], nxt[:, 384:512]]
                                X["cur_tok"] = sq_evs
                        if k == 0:
                            holder["l0"] = [P.per_engine["tensor"][-1]]
                            holder["ue0"] = list(ue0_list)
                        if k == nlev - 2 and nxt_pre is not None:
                            nxt_pre[0]()
                            pend = nxt_pre
                            nxt_pre = None
                    if nxt_pre is not None:
                        nxt_pre[0]()
                        pend = nxt_pre
                        nxt_pre = None
                    quad_prev = []
                    qp_ = q % 2
                    ox_list = []
                    oev_list = []
                    for par in ctx:
                        X = ctx[par]
                        p = X["p"]; h0 = X["h0"]
                        bo = BOX[par]
                        mmOx0 = op("tensor", lambda e, u_ap=X["u_ap"], bo=bo, par=par, qp_=qp_: e.matmul(
                            psum[bo][:, 0:64], gm2[qp_][par][:, 0:128], u_ap[:, 0:64],
                            start=True, stop=False),
                            deps=[X["evs"][("G", 0)], X["evs"][("G", 1)]] + X["u_tok"] + bankR[bo],
                            tag=f"Ox{c}_{p}0")
                        mmOx1 = op("tensor", lambda e, u_ap=X["u_ap"], bo=bo, par=par, qp_=qp_: e.matmul(
                            psum[bo][:, 64:128], gm2[qp_][par][:, 128:256], u_ap[:, 64:128],
                            start=False, stop=True), deps=[], tag=f"Ox{c}_{p}1")
                        if c > 0:
                            o_ev = op("vector", lambda e, h0=h0, bo=bo, par=par, qp_=qp_: e.scalar_tensor_tensor(
                                out=o_chunk[:, h0 * 64: h0 * 64 + 128], in0=psum[bo][:, 0:128],
                                scalar=1.0, in1=oi2q[qp_][:, par * 128:(par + 1) * 128],
                                op0=ALU.mult, op1=ALU.add),
                                deps=[mmOx1] + X["oi_ok"] + prev_o_readers, tag=f"oev{c}_{p}")
                        else:
                            o_ev = op("vector", lambda e, h0=h0, bo=bo: e.tensor_copy(
                                o_chunk[:, h0 * 64: h0 * 64 + 128], psum[bo][:, 0:128]),
                                deps=[mmOx1] + prev_o_readers, tag=f"oev{c}_{p}")
                        bankR[bo] = [o_ev]
                        ox_list += [mmOx0, mmOx1]
                        oev_list.append(o_ev)
                        dS0 = op("tensor", lambda e, k_pair=X["k_pair"], u_ap=X["u_ap"], p=p: e.matmul(
                            psum[6][0:64, p * 64:(p + 1) * 64], k_pair[:, 0:64],
                            u_ap[:, 0:64], start=(p == 0), stop=(p == NPAIR - 1),
                            tile_position=(0, 0)),
                            deps=X["u_tok"] + [s_evac, o_ev, prev_s_adds[0]]
                            + (bankR[6] if p == 0 else []), tag=f"dS{c}_{p}0")
                        dS1 = op("tensor", lambda e, k_pair=X["k_pair"], u_ap=X["u_ap"], p=p: e.matmul(
                            psum[7][64:128, p * 64:(p + 1) * 64], k_pair[:, 64:128],
                            u_ap[:, 64:128], start=(p == 0), stop=(p == NPAIR - 1),
                            tile_position=(0, 64)),
                            deps=[prev_s_adds[1]] + (bankR[7] if p == 0 else []),
                            tag=f"dS{c}_{p}1")
                        dS_ops += [dS0, dS1]
                        quad_prev += [dS0, dS1]
                    last_ox[qp_] = ox_list + oev_list
                    last_oev[qp_] = oev_list
                last_dS_pair = quad_prev[-2:]
                parts = cfg.get("scanparts", "pxdw")
                last_dS = dS_ops[-1]
                parts = cfg.get("scanparts", "pxdw")
                if "d" not in parts:
                    res_toks[c] = [last_dS, last_dS]
                    continue
                sa1 = op("vector", lambda e: e.tensor_tensor(
                    out=s32_sb[0:64, 0:(H // 2) * 64], in0=psum[6][0:64, 0:(H // 2) * 64],
                    in1=s32_sb[0:64, 0:(H // 2) * 64], op=ALU.add),
                    deps=[dS_ops[-2], dS_ops[-1]], tag=f"saddA{c}")
                sa2 = op("vector", lambda e: e.tensor_tensor(
                    out=s32_sb[64:128, 0:(H // 2) * 64], in0=psum[7][64:128, 0:(H // 2) * 64],
                    in1=s32_sb[64:128, 0:(H // 2) * 64], op=ALU.add),
                    deps=[dS_ops[-2], dS_ops[-1]], tag=f"saddB{c}")
                prev_s_adds = [sa1, sa2]
                if "w" not in parts:
                    res_toks[c] = [sa1, sa2]
                    continue
                p2bf = psum[2].ap().bitcast(BF16)
                trs_o = []
                for dc in range(NDI):
                    trs_o.append(op("tensor", lambda e, dc=dc, p2bf=p2bf: e.matmul(
                        p2bf[:, dc * 128:(dc + 1) * 128],
                        o_chunk[:, dc * 128:(dc + 1) * 128],
                        id_bf, is_transpose=True, start=(dc == 0), stop=(dc == NDI - 1)),
                        deps=([last_dS, c_id] + prev_of_readers + bankR[2])
                        if dc == 0 else [], tag=f"oT{c}_{dc}"))
                of_ops = [op("vector", lambda e, p2bf=p2bf: e.tensor_copy(
                    of_chunk[:, 0:NDI * 128], p2bf[:, 0:NDI * 128]),
                    deps=[trs_o[-1]], tag=f"oTe{c}")]
                bankR[2] = [of_ops[-1]]
                prev_o_readers = list(of_ops)
                if dbg:
                    dmp = op("gpsimd", lambda e, c=c: e.dma_start(
                        out=dbg_o[c], in_=o_chunk), deps=[last_dS], dma=True,
                        group=f"dbgo{c}", tag=f"dbgo{c}")
                    prev_o_readers = prev_o_readers + [dmp]
                res_c = []
                mmw_all = []
                for n2 in range(2):
                    bi = (3, 5)[n2]
                    mmw = None
                    for dc in range(NDI):
                        mmw = op("tensor", lambda e, dc=dc, n2=n2, bi=bi: e.matmul(
                            psum[bi][:, 0:512], of_chunk[:, dc * 128:(dc + 1) * 128],
                            wout_t[:, dc, n2 * 512:n2 * 512 + 512],
                            start=(dc == 0), stop=(dc == NDI - 1)),
                            deps=(of_ops + bankR[bi]) if dc == 0 else [],
                            tag=f"wo{c}_{n2}_{dc}")
                    mmw_all.append(mmw)
                    res_c.append(op("vector", lambda e, n2=n2, c=c, bi=bi: e.tensor_tensor(
                        out=x_sb[:, c * 1024 + n2 * 512: c * 1024 + n2 * 512 + 512],
                        in0=psum[bi][:, 0:512],
                        in1=x_sb[:, c * 1024 + n2 * 512: c * 1024 + n2 * 512 + 512],
                        op=ALU.add), deps=[mmw], tag=f"res{c}_{n2}"))
                    bankR[bi] = [res_c[-1]]
                prev_of_readers = mmw_all
                res_toks[c] = res_c
            prev_scan_join = ([last_dS] + res_toks[b * TPB + TPB - 1]
                              if mode != "proj" else [kv_last, q_evacs[-1]])

        phase_a_last = [P.per_engine[e][-1] for e in ENGINES if P.per_engine[e]]
        barrier = phase_a_last
        if mode != "full":
            outr0 = out_d.rearrange("(n p) d -> p n d", p=128)
            sA = []
            for b2 in range(NB):
                sA.append(op("sync", lambda e, b2=b2: e.dma_start(
                    out=outr0[:, b2 * TPB:(b2 + 1) * TPB, :],
                    in_=x_sb[:, b2 * TPB * 1024:(b2 + 1) * TPB * 1024].rearrange(
                        "p (n d) -> p n d", d=1024)),
                    deps=barrier, dma=True, group="store", tag=f"stA{b2}"))
            op("sync", lambda e: e.nop(), deps=sA, tag="final_joinA")
            P.emit(st)
            return nc
        a16.reset(mark16)
        CAP = cfg.get("cap", 512)
        SLT = CAP // 128
        gbuf = [a16.alloc(NDC * F) for _ in range(2)]
        ubuf = [a16.alloc(NDC * F)]
        wdbuf = [a16.alloc(NFC * D) for _ in range(2)]
        h2_stage = a16.alloc(1024)
        hftile = a16.alloc(1024)
        bd_row = [h2_stage, hftile]
        hfe = [a16.alloc(NDC * CAP) for _ in range(2)]
        gated = a16.alloc(NFC * CAP)
        g1_scr = [a16.alloc(CAP) for _ in range(2)]
        s_scr = [a16.alloc(CAP) for _ in range(2)]
        u1_scr = [a16.alloc(CAP) for _ in range(2)]
        hblk = [a16.alloc(1024) for _ in range(CAP // 128)]
        gfin_bufs = hblk[0:2]
        outst = [a16.alloc(1024) for _ in range(2)]
        gfin = gfin_bufs
        ge_all = a16.alloc(NT * E)
        onesF_bf = a16.alloc(128)
        onesU_bf = a16.alloc(128)
        em_all = a32.alloc(NT * 2 * E)
        iotaE2 = a32.alloc(2 * E)
        iotaE = a32.alloc(E)
        cwsel = a32.alloc(2 * NT)
        hfe_t = [h_.rearrange("p (c w) -> p c w", w=CAP) for h_ in hfe]
        gbuf_t = [g.rearrange("p (c f) -> p c f", f=F) for g in gbuf]
        ubuf_t = [u_.rearrange("p (c f) -> p c f", f=F) for u_ in ubuf] * 2
        wdbuf_t = [w_.rearrange("p (c f) -> p c f", f=D) for w_ in wdbuf]
        tid_i32 = st.enter_context(nc.sbuf_tensor("tid_i32", [128, NT], I32))
        dst_i32 = st.enter_context(nc.sbuf_tensor("dst_i32", [128, 2 * NT], I32))
        eix_i32 = st.enter_context(nc.sbuf_tensor("eix_i32", [128, 2 * SLT], I32))
        pre_i32 = st.enter_context(nc.sbuf_tensor("pre_i32", [128, E * CAP // 128], I32))

        c_onesF = op("gpsimd", lambda e: e.memset(onesF_bf, 1.0), deps=barrier, tag="onesF")
        c_onesU = op("vector", lambda e: e.tensor_scalar(
            out=onesU_bf, in0=iota_f, scalar1=iota_p[:, 0:1], scalar2=None,
            op0=ALU.is_gt), deps=[c_iota_f, c_iota_p] + barrier, tag="onesU")
        c_iotaE = op("gpsimd", lambda e: e.iota(iotaE, [[1, E]], channel_multiplier=0,
                     allow_small_or_imprecise_dtypes=True), deps=barrier, tag="iotaE")
        c_iotaE2 = op("gpsimd", lambda e: e.iota(iotaE2, [[0, 2], [1, E]],
                      channel_multiplier=0, allow_small_or_imprecise_dtypes=True),
                      deps=barrier, tag="iotaE2")
        c_tid = op("gpsimd", lambda e: e.iota(tid_i32[:, :], [[128, NT]],
                   channel_multiplier=1), deps=barrier, tag="tid")
        c_pre = op("gpsimd", lambda e: e.iota(pre_i32[:, :], [[0, E * CAP // 128]],
                   base=0, channel_multiplier=0), deps=barrier, tag="pre")
        pre_st = op("sync", lambda e: e.dma_start(
            out=idl_d.rearrange("(p b) -> p b", p=128), in_=pre_i32[:, :]),
            deps=[c_pre] + barrier, dma=True, group="idpre", tag="prest")

        pre_w = {}
        for e_i in range(min(2, E)):
            sl = e_i
            dw = {}
            dw["ld_g"] = op("gpsimd", lambda e, e_i=e_i, sl=sl: e.dma_start(
                out=gbuf_t[sl], in_=wgate_d[e_i]),
                deps=barrier, dma=True, group=f"wgE{sl}", tag=f"ldg{e_i}")
            dw["ld_d"] = op("gpsimd", lambda e, e_i=e_i, sl=sl: e.dma_start(
                out=wdbuf_t[sl], in_=wd_d[e_i]),
                deps=barrier, dma=True, group=f"wdE{sl}", tag=f"ldd{e_i}")
            dw["ld_bg1"] = op("sync", lambda e, e_i=e_i, sl=sl: e.dma_start(
                out=bgug_col[sl][:, :], in_=bgug_d[e_i]),
                deps=barrier, dma=True, group=f"bgEg{sl}", tag=f"ldbgg{e_i}")
            dw["ld_bg2"] = op("sync", lambda e, e_i=e_i, sl=sl: e.dma_start(
                out=bguu_col[sl][:, :], in_=bguu_d[e_i]),
                deps=barrier, dma=True, group=f"bgEu{sl}", tag=f"ldbgu{e_i}")
            if e_i == 0:
                dw["ld_u"] = op("gpsimd", lambda e: e.dma_start(
                    out=ubuf_t[0], in_=wup_d[0]),
                    deps=barrier, dma=True, group="wuE", tag="ldu0")
            pre_w[e_i] = dw

        rc = a32.alloc(40)
        lgt2 = [lgt_sb, a32.alloc(E)]
        junk2 = [a32.alloc(2 * E), a32.alloc(2 * E)]
        h2buf = [h2_stage, gated[:, 1024:2048]]
        hfbuf = [hftile, gated[:, 2048:3072]]
        dumpb = [gated[:, 0:1024], gated[:, 3072:4096]]
        h2_ready = []
        id_scat = []
        cw_ready = {}
        tq3_all = {}
        prev_trs = {0: [], 1: []}
        prev_mml = {0: [], 1: []}
        prev_scrR = {0: [], 1: []}
        prev_h2s = {0: [], 1: []}

        def route_chain(tile):
            par = tile % 2
            src_t = x_sb[:, tile * 1024:(tile + 1) * 1024]
            dst = h2buf[par]
            dump = dumpb[par]

            def col(i, w=1):
                return rc[:, par * 20 + i:par * 20 + i + w]
            ssum, ssq, mean, vpe, rstd, mrstd = [col(i) for i in range(6)]
            m1, m2, em2, dinv, nm1 = col(6), col(7), col(8), col(9), col(10)
            e12 = col(11, 2)
            pos12 = col(13, 2)
            ogc2 = col(15, 2)
            dstf2 = col(17, 2)
            scrE = lgt2[par][:, 0:E]
            junk = junk2[par]
            cwt = cw_sb[:, tile * E:(tile + 1) * E]
            emt = em_all[:, tile * 2 * E:(tile + 1) * 2 * E]
            eqm = emt[:, 0:E]
            m2m = emt[:, E:2 * E]
            gea = ge_all[:, tile * E:(tile + 1) * E]
            deps0 = barrier + res_toks[tile] + prev_h2s[par] + prev_trs[par]
            t1 = op("vector", lambda e: e.tensor_reduce(
                out=ssum, in_=src_t, axis=AX.X, op=ALU.add), deps=deps0, tag=f"r{tile}_sum")
            yield
            t2 = op("scalar", lambda e: e.activation(
                out=dump, in_=src_t, func=AF.Square, accum_out=ssq),
                deps=deps0, tag=f"r{tile}_sq")
            yield
            t3 = op("vector", lambda e: e.tensor_scalar(
                out=mean, in0=ssum, scalar1=1.0 / 1024, scalar2=None, op0=ALU.mult),
                deps=[t1], tag=f"r{tile}_mean")
            yield
            t4 = op("vector", lambda e: e.tensor_scalar(
                out=vpe, in0=mean, scalar1=mean, scalar2=None, op0=ALU.mult),
                deps=[t3], tag=f"r{tile}_m2")
            yield
            t5 = op("vector", lambda e: e.scalar_tensor_tensor(
                out=vpe, in0=ssq, scalar=1.0 / 1024, in1=vpe,
                op0=ALU.mult, op1=ALU.subtract), deps=[t2, t4], tag=f"r{tile}_var")
            yield
            t6 = op("scalar", lambda e: e.activation(
                out=rstd, in_=vpe, func=AF.Ln, bias=eps_col),
                deps=[t5, c_eps], tag=f"r{tile}_ln")
            yield
            t7 = op("scalar", lambda e: e.activation(
                out=rstd, in_=rstd, func=AF.Exp, scale=-0.5), deps=[t6], tag=f"r{tile}_exp")
            yield
            t8 = op("vector", lambda e: e.tensor_scalar(
                out=mrstd, in0=mean, scalar1=rstd, scalar2=None, op0=ALU.mult),
                deps=[t7, t3], tag=f"r{tile}_mr")
            yield
            t9 = op("vector", lambda e: e.tensor_scalar(
                out=dst, in0=src_t, scalar1=rstd, scalar2=mrstd,
                op0=ALU.mult, op1=ALU.subtract), deps=[t8, t7] + deps0, tag=f"r{tile}_nrm")
            yield
            h2s = op("sync", lambda e, tile=tile: e.dma_start(
                out=h2d_d[tile * 128:(tile + 1) * 128, :], in_=dst),
                deps=[t9], dma=True, group=f"h2s{par}", tag=f"h2s{tile}")
            h2_ready.append(h2s)
            prev_h2s[par] = [h2s]
            yield
            pT = 6 + par
            pbf = psum[pT].ap().bitcast(BF16)
            trs = []
            for dc in range(NDC):
                trs.append(op("tensor", lambda e, dc=dc, pbf=pbf: e.matmul(
                    pbf[:, dc * 128:(dc + 1) * 128], dst[:, dc * 128:(dc + 1) * 128],
                    id_bf, is_transpose=True, start=(dc == 0), stop=(dc == NDC - 1)),
                    deps=([t9, c_id] + bankR[pT]) if dc == 0 else [],
                    tag=f"h2T{tile}_{dc}"))
            prev_trs[par] = [trs[-1]]
            yield
            hfev = op("vector", lambda e, pbf=pbf: e.tensor_copy(
                hfbuf[par], pbf[:, 0:NDC * 128]), deps=[trs[-1]] + prev_mml[par],
                tag=f"h2Te{tile}")
            bankR[pT] = [hfev]
            yield
            lgb = 4 + par
            mml = None
            for dc in range(NDC):
                mml = op("tensor", lambda e, dc=dc, lgb=lgb: e.matmul(
                    psum[lgb][:, 0:E], hfbuf[par][:, dc * 128:(dc + 1) * 128],
                    wgt_t[:, dc, :], start=(dc == 0), stop=False),
                    deps=([hfev] + bankR[lgb]) if dc == 0 else [],
                    tag=f"lg{tile}_{dc}")
            mml = op("tensor", lambda e, lgb=lgb: e.matmul(
                psum[lgb][:, 0:E], ones1[0:1, :], bg_row[0:1, 0:E], start=False, stop=True),
                deps=[c_ones], tag=f"lgb{tile}")
            prev_mml[par] = [mml]
            yield
            lcp = op("vector", lambda e, lgb=lgb: e.tensor_copy(scrE, psum[lgb][:, 0:E]),
                     deps=[mml] + prev_scrR[par], tag=f"lcp{tile}")
            bankR[lgb] = [lcp]
            yield
            t1m = op("vector", lambda e: e.tensor_reduce(
                out=m1, in_=scrE, axis=AX.X, op=ALU.max), deps=[lcp], tag=f"m1_{tile}")
            yield
            t2m = op("vector", lambda e, cwt=cwt: e.tensor_scalar(
                out=cwt, in0=scrE, scalar1=m1, scalar2=-1e30,
                op0=ALU.is_equal, op1=ALU.mult), deps=[t1m], tag=f"eq_{tile}")
            yield
            t3m = op("vector", lambda e, cwt=cwt: e.tensor_tensor(
                out=cwt, in0=cwt, in1=scrE, op=ALU.add), deps=[t2m], tag=f"lm_{tile}")
            yield
            t4m = op("vector", lambda e, cwt=cwt: e.tensor_reduce(
                out=m2, in_=cwt, axis=AX.X, op=ALU.max), deps=[t3m], tag=f"m2_{tile}")
            yield
            t5m = op("vector", lambda e: e.tensor_scalar(
                out=nm1, in0=m1, scalar1=-1.0, scalar2=None, op0=ALU.mult),
                deps=[t1m], tag=f"nm1_{tile}")
            yield
            t6m = op("scalar", lambda e: e.activation(
                out=em2, in_=m2, func=AF.Exp, bias=nm1), deps=[t4m, t5m], tag=f"em2_{tile}")
            yield
            t7m = op("vector", lambda e: e.tensor_scalar(
                out=dinv, in0=em2, scalar1=1.0, scalar2=None, op0=ALU.add),
                deps=[t6m], tag=f"den_{tile}")
            yield
            t8m = op("vector", lambda e, tile=tile: e.reciprocal(
                cwsel[:, 2 * tile:2 * tile + 1], dinv), deps=[t7m], tag=f"rec_{tile}")
            yield
            c2c = op("vector", lambda e, tile=tile: e.tensor_scalar(
                out=cwsel[:, 2 * tile + 1:2 * tile + 2],
                in0=cwsel[:, 2 * tile:2 * tile + 1], scalar1=-1.0, scalar2=1.0,
                op0=ALU.mult, op1=ALU.add), deps=[t8m], tag=f"cw2_{tile}")
            cw_ready[tile] = c2c
            yield
            tq1 = op("vector", lambda e, eqm=eqm: e.tensor_scalar(
                out=eqm, in0=scrE, scalar1=m1, scalar2=None, op0=ALU.is_equal),
                deps=[t1m, t4m], tag=f"eqm_{tile}")
            yield
            t10 = op("vector", lambda e, cwt=cwt: e.tensor_scalar(
                out=cwt, in0=scrE, scalar1=m2, scalar2=None, op0=ALU.is_ge),
                deps=[t4m], tag=f"ge_{tile}")
            yield
            tq2 = op("vector", lambda e, m2m=m2m, cwt=cwt, eqm=eqm: e.tensor_tensor(
                out=m2m, in0=cwt, in1=eqm, op=ALU.subtract),
                deps=[t10, tq1], tag=f"m2m_{tile}")
            yield
            tq3 = op("vector", lambda e, gea=gea, cwt=cwt: e.tensor_scalar(
                out=gea, in0=cwt, scalar1=1.0, scalar2=None, op0=ALU.mult),
                deps=[t10], tag=f"geb_{tile}")
            tq3_all[tile] = tq3
            yield
            tq45a = op("vector", lambda e, emt=emt: e.tensor_tensor(
                out=junk, in0=emt, in1=iotaE2, op=ALU.mult),
                deps=[tq1, tq2, c_iotaE2], tag=f"e12a_{tile}")
            prev_scrR[par] = [t10]
            yield
            tq45 = op("vector", lambda e: e.tensor_reduce(
                out=e12, in_=junk.rearrange("p (g f) -> p g f", f=E),
                axis=AX.X, op=ALU.add), deps=[tq45a], tag=f"e12_{tile}")
            yield
            pB = 2 + par
            pdep = [c_onesF, c_onesU, tq3] + bankR[pB]
            if tile >= 1:
                pdep.append(tq3_all[tile - 1])
            pmm = None
            first = True
            for j in range(tile):
                for hf in range(2):
                    pmm = op("tensor", lambda e, j=j, pB=pB, hf=hf, st_=first: e.matmul(
                        psum[pB][:, hf * E:(hf + 1) * E], onesF_bf,
                        ge_all[:, j * E:(j + 1) * E],
                        start=st_, stop=False),
                        deps=pdep if first else [],
                        tag=f"pm{tile}_{j}_{hf}")
                    first = False
            for hf in range(2):
                pmm = op("tensor", lambda e, tile=tile, pB=pB, hf=hf, st_=first: e.matmul(
                    psum[pB][:, hf * E:(hf + 1) * E], onesU_bf,
                    ge_all[:, tile * E:(tile + 1) * E],
                    start=st_, stop=(hf == 1)),
                    deps=pdep if first else [],
                    tag=f"pmU{tile}_{hf}")
                first = False
            yield
            tp12a = op("vector", lambda e, emt=emt, pB=pB: e.tensor_tensor(
                out=junk, in0=emt, in1=psum[pB][:, 0:2 * E], op=ALU.mult),
                deps=[pmm, tq45], tag=f"p12a_{tile}")
            yield
            tp12 = op("vector", lambda e: e.tensor_reduce(
                out=pos12, in_=junk.rearrange("p (g f) -> p g f", f=E),
                axis=AX.X, op=ALU.add), deps=[tp12a], tag=f"p12_{tile}")
            bankR[pB] = [tp12]
            yield
            dg2 = op("vector", lambda e: e.tensor_scalar(
                out=ogc2, in0=pos12, scalar1=float(CAP - 1), scalar2=None,
                op0=ALU.min), deps=[tp12], tag=f"og{tile}")
            yield
            dg1 = op("vector", lambda e: e.scalar_tensor_tensor(
                out=dstf2, in0=e12, scalar=float(CAP), in1=ogc2,
                op0=ALU.mult, op1=ALU.add), deps=[dg2, tq45], tag=f"dd{tile}")
            yield
            dg4 = op("vector", lambda e, tile=tile: e.tensor_copy(
                dst_i32[:, 2 * tile:2 * tile + 2], dstf2),
                deps=[dg1], tag=f"di{tile}")
            yield
            for slot in range(2):
                id_scat.append(op("gpsimd", lambda e, tile=tile, slot=slot: e.indirect_dma_start(
                    out=idl_d.rearrange("(a b) -> a b", b=1),
                    out_offset=IndirectOffsetOnAxis(ap=dst_i32[:, 2 * tile + slot:2 * tile + slot + 1], axis=0),
                    in_=tid_i32[:, tile:tile + 1], in_offset=None),
                    deps=[dg4, c_tid, pre_st], dma=True, group="ids", tag=f"sc{tile}_{slot}"))
                yield

        chains = [route_chain(0)]
        nxt_tile = 1
        if NT > 1:
            chains.append(route_chain(1))
            nxt_tile = 2
        while chains:
            for g_ in list(chains):
                try:
                    next(g_)
                except StopIteration:
                    chains.remove(g_)
                    if nxt_tile < NT:
                        chains.append(route_chain(nxt_tile))
                        nxt_tile += 1

        for e_i in range(min(2, E)):
            bdep = h2_ready[-2:] if e_i == 0 else prev_mml[0] + prev_trs[0]
            pre_w[e_i]["ld_bd"] = op("gpsimd", lambda e, e_i=e_i: e.dma_start(
                out=bd_row[e_i][0:1, :], in_=bd_d[e_i].rearrange("(a b) -> a b", a=1)),
                deps=bdep, dma=True, group=f"bdE{e_i}", tag=f"ldbd{e_i}")

        last_g_rd = [None, None]
        last_u_rd = [None, None]
        last_wd_rd = [None, None]
        last_bd_rd = [None, None]
        last_bg_rd = [None, None]
        last_eix_rd = [None, None]
        last_hfe_rd = [None, None]
        last_hblk_rd = [None] * SLT
        ids_last = id_scat[-1]
        h2s_last = h2_ready[-2:]
        oed_stores = []
        outst_rd = [[], []]
        last_down_mm = None
        prevs = {0: (None, None, None), 1: (None, None, None)}
        hfe_ready = [None, None]

        def prefetch_dma(e_i):
            sl = e_i % 2
            if e_i in pre_w:
                d = pre_w[e_i]
            else:
                d = {}
                d["ld_g"] = op("gpsimd", lambda e, e_i=e_i, sl=sl: e.dma_start(
                    out=gbuf_t[sl], in_=wgate_d[e_i]),
                    deps=[last_g_rd[sl]], dma=True, group=f"wgE{sl}", tag=f"ldg{e_i}")
                d["ld_d"] = op("gpsimd", lambda e, e_i=e_i, sl=sl: e.dma_start(
                    out=wdbuf_t[sl], in_=wd_d[e_i]),
                    deps=[last_wd_rd[sl]], dma=True, group=f"wdE{sl}", tag=f"ldd{e_i}")
                d["ld_bd"] = op("gpsimd", lambda e, e_i=e_i, sl=sl: e.dma_start(
                    out=bd_row[sl][0:1, :], in_=bd_d[e_i].rearrange("(a b) -> a b", a=1)),
                    deps=[last_bd_rd[sl]], dma=True, group=f"bdE{sl}", tag=f"ldbd{e_i}")
                d["ld_bg1"] = op("sync", lambda e, e_i=e_i, sl=sl: e.dma_start(
                    out=bgug_col[sl][:, :], in_=bgug_d[e_i]),
                    deps=[last_bg_rd[sl]], dma=True, group=f"bgEg{sl}", tag=f"ldbgg{e_i}")
                d["ld_bg2"] = op("sync", lambda e, e_i=e_i, sl=sl: e.dma_start(
                    out=bguu_col[sl][:, :], in_=bguu_d[e_i]),
                    deps=[last_bg_rd[sl]], dma=True, group=f"bgEu{sl}", tag=f"ldbgu{e_i}")
            ld_ix = op("sync", lambda e, e_i=e_i, sl=sl: e.dma_start(
                out=eix_i32[:, sl * SLT:(sl + 1) * SLT],
                in_=idl_d.rearrange("(ee b p) -> ee p b", p=128, b=SLT)[e_i],
                allow_slow_non_contiguous=True),
                deps=[last_eix_rd[sl], ids_last], dma=True,
                group=f"eix{sl}", tag=f"ldix{e_i}")
            d["gg"] = []
            for blk in range(SLT):
                d["gg"].append(op("gpsimd", lambda e, sl=sl, blk=blk: e.indirect_dma_start(
                    out=hblk[blk], out_offset=None, in_=h2d_d,
                    in_offset=IndirectOffsetOnAxis(
                        ap=eix_i32[:, sl * SLT + blk:sl * SLT + blk + 1], axis=0)),
                    deps=[ld_ix, last_hblk_rd[blk]] + h2s_last,
                    dma=True, group=f"hg{blk}", tag=f"hg{e_i}_{blk}"))
            last_eix_rd[sl] = d["gg"][-1]
            return d

        def prefetch_T_gen(e_i, d, out):
            sl = e_i % 2
            for blk in range(SLT):
                pbi = 6 + (blk % 2)
                pbf = psum[pbi].ap().bitcast(BF16)
                trs_g = []
                for dc in range(NDC):
                    trs_g.append(op("tensor", lambda e, dc=dc, blk=blk, pbf=pbf: e.matmul(
                        pbf[:, dc * 128:(dc + 1) * 128], hblk[blk][:, dc * 128:(dc + 1) * 128],
                        id_bf, is_transpose=True, start=(dc == 0), stop=(dc == NDC - 1)),
                        deps=([d["gg"][blk], c_id] + bankR[pbi]) if dc == 0 else [],
                        tag=f"gT{e_i}_{blk}_{dc}"))
                last_hblk_rd[blk] = trs_g[-1]
                hfe_done = op("vector", lambda e, blk=blk, pbf=pbf, sl=sl: e.tensor_copy(
                    hfe_t[sl][:, :, blk * 128:(blk + 1) * 128], pbf[:, 0:NDC * 128].rearrange(
                        "p (c w) -> p c w", w=128)),
                    deps=[trs_g[-1], last_hfe_rd[sl]], tag=f"gTe{e_i}_{blk}")
                bankR[pbi] = [hfe_done]
                out[0] = hfe_done
                yield

        def prefetch_T(e_i, d):
            out = [None]
            for _ in prefetch_T_gen(e_i, d, out):
                pass
            return out[0]

        def emit_ld_u(e_i, extra):
            return op("gpsimd", lambda e, e_i=e_i: e.dma_start(
                out=ubuf_t[0], in_=wup_d[e_i]),
                deps=[last_u_rd[0]] + extra, dma=True, group="wuE", tag=f"ldu{e_i}")

        pfd = {0: prefetch_dma(0)}
        hfe_ready[0] = prefetch_T(0, pfd[0])
        for e_i in range(E):
            sl = e_i % 2
            d = pfd.pop(e_i)
            if e_i + 1 < E:
                pfd[e_i + 1] = prefetch_dma(e_i + 1)
            ld_g, ld_u, ld_d = d["ld_g"], d["ld_u"], d["ld_d"]
            ld_bd, ld_bg1, ld_bg2 = d["ld_bd"], d["ld_bg1"], d["ld_bg2"]
            gated_done = []
            for j in range(NFC):
                bgi = j % 2
                bui = 2 + (j % 2)
                mmg = mmu = None
                for dc in range(NDC):
                    mmg = op("tensor", lambda e, dc=dc, j=j, sl=sl, bgi=bgi: e.matmul(
                        psum[bgi][:, 0:CAP], gbuf_t[sl][:, dc, j * 128:(j + 1) * 128],
                        hfe_t[sl][:, dc, :], start=(dc == 0), stop=(dc == NDC - 1)),
                        deps=([ld_g, hfe_ready[sl]] + bankR[bgi]) if dc == 0 else [],
                        tag=f"g{e_i}_{j}_{dc}")
                    mmu = op("tensor", lambda e, dc=dc, j=j, bui=bui, sl=sl: e.matmul(
                        psum[bui][:, 0:CAP], ubuf_t[sl][:, dc, j * 128:(j + 1) * 128],
                        hfe_t[sl][:, dc, :], start=(dc == 0), stop=(dc == NDC - 1)),
                        deps=([ld_u] + bankR[bui]) if dc == 0 else [],
                        tag=f"u{e_i}_{j}_{dc}")
                last_g_rd[sl] = mmg
                last_u_rd[0] = mmu
                last_hfe_rd[sl] = mmu
                pj = j % 2
                p2, p4, p5 = prevs[pj]
                e1 = op("vector", lambda e, j=j, sl=sl, bgi=bgi, pj=pj: e.tensor_scalar(
                    out=g1_scr[pj], in0=psum[bgi][:, 0:CAP], scalar1=bgug_col[sl][:, j:j + 1],
                    scalar2=7.0, op0=ALU.add, op1=ALU.min),
                    deps=[mmg, ld_bg1, p2, p4], tag=f"eg{e_i}_{j}")
                bankR[bgi] = [e1]
                e2 = op("scalar", lambda e, pj=pj: e.activation(
                    out=s_scr[pj], in_=g1_scr[pj], func=AF.Sigmoid, scale=1.702),
                    deps=[e1, p4], tag=f"es{e_i}_{j}")
                e3 = op("scalar", lambda e, j=j, sl=sl, bui=bui, pj=pj: e.activation(
                    out=u1_scr[pj], in_=psum[bui][:, 0:CAP], func=AF.Identity,
                    bias=bguu_col[sl][:, j:j + 1]),
                    deps=[mmu, ld_bg2, p5], tag=f"eu{e_i}_{j}")
                bankR[bui] = [e3]
                e4 = op("vector", lambda e, pj=pj: e.tensor_tensor(
                    out=g1_scr[pj], in0=g1_scr[pj], in1=s_scr[pj], op=ALU.mult),
                    deps=[e2, e1], tag=f"eglu{e_i}_{j}")
                e5 = op("vector", lambda e, pj=pj: e.tensor_scalar(
                    out=u1_scr[pj], in0=u1_scr[pj], scalar1=7.0, scalar2=-7.0,
                    op0=ALU.min, op1=ALU.max), deps=[e3], tag=f"ecl{e_i}_{j}")
                e6 = op("vector", lambda e, j=j, pj=pj: e.scalar_tensor_tensor(
                    out=gated[:, j * CAP:(j + 1) * CAP], in0=u1_scr[pj], scalar=1.0,
                    in1=g1_scr[pj], op0=ALU.add, op1=ALU.mult),
                    deps=[e5, e4] + ([last_down_mm] if e_i > 0 and j < 2 else []),
                    tag=f"egd{e_i}_{j}")
                last_bg_rd[sl] = e3
                gated_done.append(e6)
                prevs[pj] = (e2, e4, e6)
            if e_i + 1 < E:
                pfd[e_i + 1]["ld_u"] = emit_ld_u(e_i + 1, [])
            tout = [None]
            tgen = (prefetch_T_gen(e_i + 1, pfd[e_i + 1], tout)
                    if e_i + 1 < E else None)
            for stt_ in range(SLT):
                if tgen is not None:
                    try:
                        next(tgen)
                    except StopIteration:
                        tgen = None
                mmd_h = {}
                for fc in range(NFC):
                    for dh in range(2):
                        bi = 4 + dh
                        mmd_h[dh] = op("tensor", lambda e, fc=fc, stt_=stt_, dh=dh, bi=bi, sl=sl: e.matmul(
                            psum[bi][:, 0:512],
                            gated[:, fc * CAP + stt_ * 128: fc * CAP + stt_ * 128 + 128],
                            wdbuf_t[sl][:, fc, dh * 512:dh * 512 + 512],
                            start=(fc == 0), stop=False),
                            deps=([gated_done[fc], ld_d] + bankR[bi]) if fc == 0
                            else [gated_done[fc]],
                            tag=f"d{e_i}_{stt_}_{dh}_{fc}")
                last_wd_rd[sl] = mmd_h[1]
                evs_d = []
                for dh in range(2):
                    bi = 4 + dh
                    mmdb = op("tensor", lambda e, dh=dh, bi=bi, sl=sl: e.matmul(
                        psum[bi][:, 0:512], ones1[0:1, :],
                        bd_row[sl][0:1, dh * 512:dh * 512 + 512],
                        start=False, stop=True), deps=[ld_bd, c_ones],
                        tag=f"db{e_i}_{stt_}_{dh}")
                    last_bd_rd[sl] = mmdb
                    if dh == 0:
                        evs_d.append(op("vector", lambda e, stt_=stt_, bi=bi: e.tensor_copy(
                            outst[stt_ % 2][:, 0:512], psum[bi][:, 0:512]),
                            deps=[mmdb] + outst_rd[stt_ % 2], tag=f"de{e_i}_{stt_}_0"))
                    else:
                        evs_d.append(op("scalar", lambda e, stt_=stt_, bi=bi: e.activation(
                            out=outst[stt_ % 2][:, 512:1024], in_=psum[bi][:, 0:512],
                            func=AF.Copy), deps=[mmdb] + outst_rd[stt_ % 2],
                            tag=f"de{e_i}_{stt_}_1"))
                    bankR[bi] = [evs_d[-1]]
                st_o = op("sync", lambda e, e_i=e_i, stt_=stt_: e.dma_start(
                    out=oed_d[e_i * CAP + stt_ * 128: e_i * CAP + (stt_ + 1) * 128, :],
                    in_=outst[stt_ % 2]), deps=evs_d, dma=True, group=f"oes{stt_ % 2}",
                    tag=f"oes{e_i}_{stt_}")
                oed_stores.append(st_o)
                outst_rd[stt_ % 2] = [st_o]
                last_down_mm = mmd_h[1]
            if tgen is not None:
                for _ in tgen:
                    pass
            if e_i + 1 < E:
                hfe_ready[(e_i + 1) % 2] = tout[0]

        if dbg:
            dmp1 = op("sync", lambda e: e.dma_start(
                out=dbg_idl.rearrange("(p b) -> p b", p=128),
                in_=idl_d.rearrange("(p b) -> p b", p=128)), deps=id_scat, dma=True,
                group="dbgi", tag="dbgidl")
            op("sync", lambda e: e.dma_start(out=dbg_dst, in_=dst_i32[:, :]),
               deps=[d for (t, d) in [] ] + id_scat, dma=True, group="dbgd", tag="dbgdst")
            op("gpsimd", lambda e: e.dma_start(out=dbg_oed, in_=oed_d),
               deps=oed_stores, dma=True, group="dbgo2", tag="dbgoed")
        oes_last = oed_stores[-2:]
        fin_ops = []
        gf_rd = [[], []]
        for tile in range(NT):
            combs = []
            for slot in range(2):
                gb = gfin[slot]
                gg = op("gpsimd", lambda e, gb=gb, tile=tile, slot=slot: e.indirect_dma_start(
                    out=gb, out_offset=None, in_=oed_d,
                    in_offset=IndirectOffsetOnAxis(
                        ap=dst_i32[:, 2 * tile + slot:2 * tile + slot + 1], axis=0)),
                    deps=gf_rd[slot] + oes_last, dma=True, group=f"fg{slot}",
                    tag=f"fg{tile}_{slot}")
                cc = op("vector", lambda e, gb=gb, tile=tile, slot=slot: e.scalar_tensor_tensor(
                    out=x_sb[:, tile * 1024:(tile + 1) * 1024], in0=gb,
                    scalar=cwsel[:, 2 * tile + slot:2 * tile + slot + 1],
                    in1=x_sb[:, tile * 1024:(tile + 1) * 1024],
                    op0=ALU.mult, op1=ALU.add),
                    deps=[gg, cw_ready[tile]] + combs, tag=f"fc{tile}_{slot}")
                gf_rd[slot] = [cc]
                combs.append(cc)
            fin_ops.append((tile, combs[-1]))

        outr = out_d.rearrange("(n p) d -> p n d", p=128)
        st_ops = []
        for b2 in range(NB):
            deps_b = [d for (t, d) in fin_ops if b2 * TPB <= t < (b2 + 1) * TPB]
            st_ops.append(op("sync", lambda e, b2=b2: e.dma_start(
                out=outr[:, b2 * TPB:(b2 + 1) * TPB, :],
                in_=x_sb[:, b2 * TPB * 1024:(b2 + 1) * TPB * 1024].rearrange(
                    "p (n d) -> p n d", d=1024)),
                deps=deps_b, dma=True, group="store", tag=f"st{b2}"))
        op("sync", lambda e: e.nop(), deps=st_ops, tag="final_join")

        assert a16.peak <= 67760, (a16.peak, a32.peak)
        assert a32.peak <= 19200, a32.peak
        P.emit(st)
    return nc


CFG_FULL = dict(T=2048, H=16, E=16)


def prep_inputs(L0_attn_x, L0_attn_Wq, L0_attn_Wk, L0_attn_Wv, L0_attn_Wbeta,
                L0_attn_b_beta, L0_attn_Wout, L0_ffn_Wg, L0_ffn_bg, L0_ffn_Wgu,
                L0_ffn_bgu, L0_ffn_Wd, L0_ffn_bd, cfg):
    T, H, E = cfg["T"], cfg["H"], cfg["E"]
    DI = H * 64
    c = np.ascontiguousarray
    f = np.asarray
    Wq = f(L0_attn_Wq); Wk = f(L0_attn_Wk); Wv = f(L0_attn_Wv)
    Wb = f(L0_attn_Wbeta); Wout = f(L0_attn_Wout)
    Wg = f(L0_ffn_Wg); Wgu = f(L0_ffn_Wgu); Wd = f(L0_ffn_Wd)
    bgu = f(L0_ffn_bgu)
    w = {
        "wq": c(Wq[:, :DI].reshape(8, 128, DI).transpose(1, 0, 2)),
        "wk": c(Wk[:, :DI].reshape(8, 128, DI).transpose(1, 0, 2)),
        "wv": c(Wv[:, :DI].reshape(8, 128, DI).transpose(1, 0, 2)),
        "wout": c(Wout[:DI].reshape(DI // 128, 128, 1024).transpose(1, 0, 2)),
        "wbeta": c(Wb[:, :H].reshape(8, 128, H).transpose(1, 0, 2)),
        "bbeta": c(f(L0_attn_b_beta)[:H]),
        "wgt": c(Wg[:E].T.reshape(8, 128, E).transpose(1, 0, 2)),
        "bg": c(f(L0_ffn_bg)[:E]),
        "wgate": c(Wgu[:E, :, 0::2].reshape(E, 8, 128, 1024).transpose(0, 2, 1, 3)),
        "wup": c(Wgu[:E, :, 1::2].reshape(E, 8, 128, 1024).transpose(0, 2, 1, 3)),
        "wd": c(Wd[:E].reshape(E, 8, 128, 1024).transpose(0, 2, 1, 3)),
        "bgug": c(bgu[:E, 0::2].reshape(E, 8, 128).transpose(0, 2, 1)),
        "bguu": c(bgu[:E, 1::2].reshape(E, 8, 128).transpose(0, 2, 1)),
        "bd": c(f(L0_ffn_bd)[:E]),
    }
    xs = np.asarray(L0_attn_x)
    return w, xs


def kernel(**inputs):
    cfg = CFG_FULL
    w, xs = prep_inputs(cfg=cfg, **inputs)
    nc = build(cfg)
    in_maps = []
    for i in range(8):
        m = dict(w)
        m["x"] = np.ascontiguousarray(xs[i, :cfg["T"], :])
        in_maps.append(m)
    res = run_bass_kernel_spmd(nc, in_maps, core_ids=list(range(8)))
    return np.stack([np.asarray(res.results[i]["out"]) for i in range(8)], axis=0)


# version 28
# speedup vs baseline: 1.0102x; 1.0102x over previous
import contextlib
import numpy as np
import concourse.bass as bass
from concourse.bass import IndirectOffsetOnAxis
import concourse.mybir as mybir
from concourse.bass_utils import run_bass_kernel_spmd

F32 = mybir.dt.float32
BF16 = mybir.dt.bfloat16
I32 = mybir.dt.int32
AF = mybir.ActivationFunctionType
ALU = mybir.AluOpType
AX = mybir.AxisListType

ENGINES = ("sync", "tensor", "vector", "scalar", "gpsimd")
SERIAL = ("vector", "scalar", "gpsimd")


class _Op:
    __slots__ = ("engine", "fn", "deps", "dma", "group", "needs_inc", "sem_val",
                 "tag", "idx")

    def __init__(self, engine, fn, deps, dma, group, tag):
        self.engine = engine
        self.fn = fn
        self.deps = deps
        self.dma = dma
        self.group = group
        self.needs_inc = False
        self.sem_val = None
        self.tag = tag
        self.idx = None


class Prog:

    def __init__(self, nc):
        self.nc = nc
        self.ops = []
        self.per_engine = {e: [] for e in ENGINES}
        self.groups = {}
        self._last_compute = {e: None for e in ENGINES}

    def op(self, engine, fn, deps=(), dma=False, group=None, tag=""):
        if dma and group is None:
            group = f"q_{engine}"
        deps = [d for d in deps if d is not None]
        if engine in SERIAL and not dma and self._last_compute[engine] is not None:
            deps.append(self._last_compute[engine])
        o = _Op(engine, fn, deps, dma, group if dma else None, tag)
        o.idx = len(self.per_engine[engine])
        self.per_engine[engine].append(o)
        self.ops.append(o)
        if dma:
            self.groups.setdefault(o.group, []).append(o)
        elif engine in SERIAL:
            self._last_compute[engine] = o
        return o

    def emit(self, stack):
        nc = self.nc
        for o in self.ops:
            if o.dma:
                o.needs_inc = True
            for d in o.deps:
                d.needs_inc = True
        counters = {e: 0 for e in ENGINES}
        gcounters = {g: 0 for g in self.groups}
        for e in ENGINES:
            for o in self.per_engine[e]:
                if not o.needs_inc:
                    continue
                if o.dma:
                    gcounters[o.group] += 16
                    o.sem_val = gcounters[o.group]
                else:
                    counters[e] += 1
                    o.sem_val = counters[e]
        sems = {e: stack.enter_context(nc.semaphore(f"tl_{e}")) for e in ENGINES}
        gsems = {g: stack.enter_context(nc.semaphore(f"g_{g}")) for g in self.groups}

        def sem_of(o):
            return gsems[o.group] if o.dma else sems[o.engine]

        def key_of(o):
            return ("g", o.group) if o.dma else ("e", o.engine)

        with nc.Block() as blk:
            for e in ENGINES:
                ops_e = self.per_engine[e]
                if not ops_e:
                    continue

                def make_body(e=e, ops_e=ops_e):
                    def body(eng):
                        watermark = {}
                        for o in ops_e:
                            needed = {}
                            for d in o.deps:
                                if d.engine == o.engine and not d.dma and d.idx >= o.idx:
                                    raise RuntimeError(f"bad dep {d.tag}->{o.tag}")
                                if d.sem_val > watermark.get(key_of(d), 0):
                                    cur = needed.get(key_of(d), (None, 0))[1]
                                    needed[key_of(d)] = (sem_of(d), max(cur, d.sem_val))
                            for k, (s, v) in sorted(needed.items()):
                                eng.wait_ge(s, v)
                                watermark[k] = v
                            inst = o.fn(eng)
                            if o.needs_inc:
                                inst.then_inc(sem_of(o), 16 if o.dma else 1)
                    return body

                getattr(blk, e)(make_body())


class Arena:
    def __init__(self, tensor):
        self.t = tensor
        self.off = 0
        self.peak = 0

    def alloc(self, n):
        off = (self.off + 15) // 16 * 16
        self.off = off + n
        self.peak = max(self.peak, self.off)
        return self.t[:, off:off + n]

    def mark(self):
        return self.off

    def reset(self, off):
        self.off = off


def build(cfg):
    T = cfg["T"]; H = cfg["H"]; E = cfg["E"]
    D = 1024; DH = 64
    DI = H * DH
    NDC = D // 128
    NDI = DI // 128
    NT = T // 128
    BLK = min(512, T)
    TPB = BLK // 128
    NB = T // BLK
    TW = BLK
    WKV = min(512, DI)
    NKV = DI // WKV
    F = 1024
    NFC = F // 128

    nc = bass.Bass("TRN2", target_bir_lowering=False)
    def din(name, shape):
        return nc.dram_tensor(name, shape, F32, kind="ExternalInput").ap()
    x_d = din("x", [T, D])
    wq_d = din("wq", [128, NDC, DI])
    wk_d = din("wk", [128, NDC, DI])
    wv_d = din("wv", [128, NDC, DI])
    wout_d = din("wout", [128, NDI, D])
    wbeta_d = din("wbeta", [128, NDC, H])
    bbeta_d = din("bbeta", [H])
    wgt_d = din("wgt", [128, NDC, E])
    bg_d = din("bg", [E])
    wgate_d = din("wgate", [E, 128, NDC, F])
    wup_d = din("wup", [E, 128, NDC, F])
    wd_d = din("wd", [E, 128, NFC, D])
    bgug_d = din("bgug", [E, 128, NFC])
    bguu_d = din("bguu", [E, 128, NFC])
    bd_d = din("bd", [E, D])
    out_d = nc.dram_tensor("out", [T, D], F32, kind="ExternalOutput").ap()
    CAP0 = cfg.get("cap", 512)
    h2d_d = nc.dram_tensor("h2d", [T, D], BF16, kind="Internal").ap()
    idl_d = nc.dram_tensor("idl", [E * CAP0], I32, kind="Internal").ap()
    oed_d = nc.dram_tensor("oed", [E * CAP0, D], BF16, kind="Internal").ap()
    dbg = cfg.get("dbg", False)
    if dbg:
        dbg_x2 = nc.dram_tensor("dbg_x2", [128, NT * 1024], F32, kind="ExternalOutput").ap()
        dbg_cw = nc.dram_tensor("dbg_cw", [128, NT * E], F32, kind="ExternalOutput").ap()
        dbg_beta = nc.dram_tensor("dbg_beta", [128, NT * H], F32, kind="ExternalOutput").ap()
        dbg_o = nc.dram_tensor("dbg_o", [NT, 128, DI], F32, kind="ExternalOutput").ap()
        dbg_q = nc.dram_tensor("dbg_q", [128, NDI * 128], F32, kind="ExternalOutput").ap()
        dbg_kt = nc.dram_tensor("dbg_kt", [128, DI], F32, kind="ExternalOutput").ap()
        dbg_u = nc.dram_tensor("dbg_u", [128, 64], F32, kind="ExternalOutput").ap()
        dbg_idl = nc.dram_tensor("dbg_idl", [E * cfg.get("cap", 512)], I32, kind="ExternalOutput").ap()
        dbg_dst = nc.dram_tensor("dbg_dst", [128, 2 * NT], I32, kind="ExternalOutput").ap()
        dbg_oed = nc.dram_tensor("dbg_oed", [E * cfg.get("cap", 512), 1024], F32, kind="ExternalOutput").ap()

    mode = cfg.get("mode", "full")
    st = contextlib.ExitStack()
    with st:
        a32 = Arena(st.enter_context(nc.sbuf_tensor("arena32", [128, 19200], F32)))
        a16 = Arena(st.enter_context(nc.sbuf_tensor("arena16", [128, 67760], BF16)))
        psum = [st.enter_context(nc.psum_tensor(f"pb{i}", [128, 512], F32))
                for i in range(8)]
        P = Prog(nc)
        op = P.op

        x_sb = a32.alloc(NT * 1024)
        beta_tok = a32.alloc(NT * H)
        nbeta_tok = a32.alloc(NT * H)
        cw_sb = a32.alloc(NT * E)
        maskL = a32.alloc(128)
        maskUn = a32.alloc(128)
        maskUi = a32.alloc(128)
        iota_f = a32.alloc(128)
        iota_p = a32.alloc(1)
        colscr = a32.alloc(24)
        rn_sb = a32.alloc(2 * max(H, 16))
        lgt_sb = a32.alloc(E)
        eps_col = a32.alloc(1)
        bgug_col = [a32.alloc(NFC) for _ in range(2)]
        s32_sb = a32.alloc(512)
        bguu_col = [a32.alloc(NFC) for _ in range(2)]
        id_bf = a16.alloc(128)
        ones1 = a16.alloc(128)
        bbeta_row = a16.alloc(16)
        bg_row = a16.alloc(16)
        wgt_sb = a16.alloc(NDC * E)

        mark16 = a16.mark()
        sq_scr = a16.alloc(1024)
        sq_scr2 = a16.alloc(1024)
        wq_sb = a16.alloc(NDC * DI)
        wk_sb = a16.alloc(NDC * DI)
        wv_sb = a16.alloc(NDC * DI)
        wout_sb = a16.alloc(NDI * D)
        wbeta_sb = a16.alloc(NDC * H)
        h_stage = [a16.alloc(1024)]
        hf_blk = a16.alloc(NDC * BLK)
        qf_blk = a16.alloc(NDI * BLK)
        ktok_blk = a16.alloc(TPB * DI)
        kf_blk = a16.alloc(NDI * BLK)
        vtok_blk = a16.alloc(TPB * DI)
        o_chunk = a16.alloc(DI)
        of_chunk = a16.alloc(NDI * 128)
        m_pair = [a16.alloc(256) for _ in range(2)]
        mt_pair = [a16.alloc(256) for _ in range(2)]
        gm2 = [[a16.alloc(256) for _ in range(2)] for _ in range(2)]
        r_pair = [a16.alloc(128) for _ in range(2)]
        d1_pair = [a16.alloc(128) for _ in range(2)]
        curbuf = [[a16.alloc(512) for _ in range(2)] for _ in range(2)]
        upair = [[a16.alloc(128) for _ in range(2)] for _ in range(2)]
        oi2q = [a16.alloc(256) for _ in range(2)]
        sf_bf = a16.alloc(512)

        c_iota_f = op("gpsimd", lambda e: e.iota(iota_f, [[1, 128]], channel_multiplier=0,
                      allow_small_or_imprecise_dtypes=True), tag="iota_f")
        c_iota_p = op("gpsimd", lambda e: e.iota(iota_p, [[0, 1]], channel_multiplier=1,
                      allow_small_or_imprecise_dtypes=True), tag="iota_p")
        c_maskL = op("vector", lambda e: e.tensor_scalar(out=maskL, in0=iota_f,
                     scalar1=iota_p[:, 0:1], scalar2=None, op0=ALU.is_lt),
                     deps=[c_iota_f, c_iota_p], tag="maskL")
        c_maskUn = op("vector", lambda e: e.tensor_scalar(out=maskUn, in0=iota_f,
                      scalar1=iota_p[:, 0:1], scalar2=-1.0, op0=ALU.is_gt, op1=ALU.mult),
                      deps=[c_iota_f, c_iota_p], tag="maskUn")
        c_maskUi = op("vector", lambda e: e.tensor_scalar(out=maskUi, in0=iota_f,
                      scalar1=iota_p[:, 0:1], scalar2=None, op0=ALU.is_ge),
                      deps=[c_iota_f, c_iota_p], tag="maskUi")
        c_id = op("vector", lambda e: e.tensor_scalar(out=id_bf, in0=iota_f,
                  scalar1=iota_p[:, 0:1], scalar2=None, op0=ALU.is_equal),
                  deps=[c_iota_f, c_iota_p], tag="id_bf")
        c_ones = op("gpsimd", lambda e: e.memset(ones1[0:1, :], 1.0), tag="ones1")
        c_eps = op("gpsimd", lambda e: e.memset(eps_col, 1e-5), tag="eps")
        c_s0 = op("gpsimd", lambda e: e.memset(s32_sb, 0.0), tag="s0")

        ldw = []
        for name, dst, src in (("wq", wq_sb, wq_d), ("wk", wk_sb, wk_d),
                               ("wv", wv_sb, wv_d), ("wout", wout_sb, wout_d),
                               ("wbeta", wbeta_sb, wbeta_d), ("wgt", wgt_sb, wgt_d)):
            n2 = src.shape[2]
            ldw.append(op("gpsimd", lambda e, dst=dst, src=src, n2=n2: e.dma_start(
                out=dst.rearrange("p (c f) -> p c f", f=n2), in_=src),
                dma=True, group="wconst", tag=f"ld_{name}"))
        ldw.append(op("gpsimd", lambda e: e.dma_start(
            out=bbeta_row[0:1, 0:H], in_=bbeta_d.rearrange("(a b) -> a b", a=1)),
            dma=True, group="wconst", tag="ld_bbr"))
        ldw.append(op("gpsimd", lambda e: e.dma_start(
            out=bg_row[0:1, 0:E], in_=bg_d.rearrange("(a b) -> a b", a=1)),
            dma=True, group="wconst", tag="ld_bgr"))
        wconst_last = ldw[-1]
        xr = x_d.rearrange("(n p) d -> p n d", p=128)
        ld_x = []
        for b in range(NB):
            ld_x.append(op("sync", lambda e, b=b: e.dma_start(
                out=x_sb[:, b * TPB * 1024:(b + 1) * TPB * 1024].rearrange(
                    "p (n d) -> p n d", d=1024),
                in_=xr[:, b * TPB:(b + 1) * TPB, :]),
                dma=True, group="ldx", tag=f"ld_x{b}"))
        ldx_last = ld_x[-1]

        sqread = [None, None]
        ln_prev = {"t5": None, "t9": None}

        def layernorm(src_slice, dst_slice, deps, tagp, dump=None):
            if dump is None:
                dump = sq_scr
            ssum = colscr[:, 0:1]; ssq = colscr[:, 1:2]; mean = colscr[:, 2:3]
            vpe = colscr[:, 3:4]; rstd = colscr[:, 4:5]; mrstd = colscr[:, 5:6]
            t1 = op("vector", lambda e: e.tensor_reduce(
                out=ssum, in_=src_slice, axis=AX.X, op=ALU.add), deps=deps, tag=f"{tagp}_sum")
            t2 = op("scalar", lambda e: e.activation(
                out=dump, in_=src_slice, func=AF.Square, accum_out=ssq),
                deps=deps + [sqread[0], ln_prev["t5"]], tag=f"{tagp}_sq")
            t3 = op("vector", lambda e: e.tensor_scalar(
                out=mean, in0=ssum, scalar1=1.0 / 1024, scalar2=None, op0=ALU.mult),
                deps=[t1], tag=f"{tagp}_mean")
            t4 = op("vector", lambda e: e.tensor_scalar(
                out=vpe, in0=mean, scalar1=mean, scalar2=None, op0=ALU.mult),
                deps=[t3], tag=f"{tagp}_m2")
            t5 = op("vector", lambda e: e.scalar_tensor_tensor(
                out=vpe, in0=ssq, scalar=1.0 / 1024, in1=vpe,
                op0=ALU.mult, op1=ALU.subtract), deps=[t2, t4], tag=f"{tagp}_var")
            t6 = op("scalar", lambda e: e.activation(
                out=rstd, in_=vpe, func=AF.Ln, bias=eps_col),
                deps=[t5, ln_prev["t9"], c_eps], tag=f"{tagp}_ln")
            t7 = op("scalar", lambda e: e.activation(
                out=rstd, in_=rstd, func=AF.Exp, scale=-0.5), deps=[t6], tag=f"{tagp}_exp")
            t8 = op("vector", lambda e: e.tensor_scalar(
                out=mrstd, in0=mean, scalar1=rstd, scalar2=None, op0=ALU.mult),
                deps=[t7, t3], tag=f"{tagp}_mr")
            t9 = op("vector", lambda e: e.tensor_scalar(
                out=dst_slice, in0=src_slice, scalar1=rstd, scalar2=mrstd,
                op0=ALU.mult, op1=ALU.subtract), deps=[t8, t7] + deps, tag=f"{tagp}_nrm")
            ln_prev["t5"] = t5
            ln_prev["t9"] = t9
            return t9

        hf_t = hf_blk.rearrange("p (c w) -> p c w", w=BLK)
        qf_t = qf_blk.rearrange("p (c w) -> p c w", w=BLK)
        kf_t = kf_blk.rearrange("p (c w) -> p c w", w=BLK)
        wq_t = wq_sb.rearrange("p (c f) -> p c f", f=DI)
        wk_t = wk_sb.rearrange("p (c f) -> p c f", f=DI)
        wv_t = wv_sb.rearrange("p (c f) -> p c f", f=DI)
        wout_t = wout_sb.rearrange("p (c f) -> p c f", f=D)
        wbeta_t = wbeta_sb.rearrange("p (c f) -> p c f", f=H)
        wgt_t = wgt_sb.rearrange("p (c f) -> p c f", f=E)

        bankR = [[] for _ in range(8)]

        kt_last = [None]
        dbg_pending = []
        prev_hT = []
        last_dS = None
        last_dS_pair = []
        last_ox = {0: [], 1: []}
        last_oev = {0: [], 1: []}
        prev_s_adds = [c_s0, c_s0]
        res_toks = {}
        prev_scan_join = []
        prev_of_readers = []
        prev_o_readers = []

        for b in range(NB):
            blk_dep = [ldx_last, wconst_last]
            prev_scan = list(prev_scan_join)
            hf_ready = []
            for ti in range(TPB):
                tile = b * TPB + ti
                hst = h_stage[0]
                war = hf_ready[-1:] if ti >= 1 else list(prev_hT)
                ln = layernorm(x_sb[:, tile * 1024:(tile + 1) * 1024], hst,
                               blk_dep + war + prev_scan, f"ln1_{tile}")
                p4bf = psum[4].ap().bitcast(BF16)
                trs_h = []
                for dc in range(NDC):
                    trs_h.append(op("tensor", lambda e, dc=dc, hst=hst, p4bf=p4bf: e.matmul(
                        p4bf[:, dc * 128:(dc + 1) * 128], hst[:, dc * 128:(dc + 1) * 128],
                        id_bf, is_transpose=True, start=(dc == 0), stop=(dc == NDC - 1)),
                        deps=([ln, c_id] + prev_scan + bankR[4]) if dc == 0 else [],
                        tag=f"hT{tile}_{dc}"))
                hf_ready.append(op("vector", lambda e, ti=ti, p4bf=p4bf: e.tensor_copy(
                    hf_t[:, :, ti * 128:(ti + 1) * 128], p4bf[:, 0:NDC * 128].rearrange(
                        "p (c w) -> p c w", w=128)),
                    deps=[trs_h[-1]], tag=f"hTe{tile}"))
                bankR[4] = [hf_ready[-1]]
            hf_last = hf_ready[-1]
            prev_hT = hf_ready[-1:]
            q_evacs = []
            for m in range(NDI):
                bi = 2 + (m % 2)
                bank = psum[bi]
                mm0 = None
                for dc in range(NDC):
                    mm = op("tensor", lambda e, m=m, dc=dc, bank=bank: e.matmul(
                        bank[:, 0:TW], wq_t[:, dc, m * 128:(m + 1) * 128],
                        hf_t[:, dc, 0:TW], start=(dc == 0), stop=(dc == NDC - 1)),
                        deps=([hf_last] + bankR[bi] if dc == 0 else []),
                        tag=f"q{b}_{m}_{dc}")
                q_evacs.append(op("scalar", lambda e, m=m, bank=bank: e.activation(
                    out=qf_t[:, m, 0:TW], in_=bank[:, 0:TW], func=AF.Copy),
                    deps=[mm], tag=f"qe{b}_{m}"))
                bankR[bi] = [q_evacs[-1]]
            kv_done = []
            dep0 = [hf_last, q_evacs[-1]] + prev_scan
            first_b = True
            for ti in range(TPB):
                for dc in range(NDC):
                    op("tensor", lambda e, dc=dc, ti=ti, st_=first_b: e.matmul(
                        psum[2][:, ti * H:(ti + 1) * H],
                        hf_t[:, dc, ti * 128:(ti + 1) * 128],
                        wbeta_t[:, dc, :], start=st_, stop=False),
                        deps=(dep0 + bankR[2]) if first_b else [], tag=f"bt{b}_{ti}_{dc}")
                    first_b = False
            mmb = None
            for ti in range(TPB):
                mmb = op("tensor", lambda e, ti=ti: e.matmul(
                    psum[2][:, ti * H:(ti + 1) * H], ones1[0:1, :], bbeta_row[0:1, 0:H],
                    start=False, stop=(ti == TPB - 1)), deps=[c_ones], tag=f"btb{b}_{ti}")
            bsig = op("scalar", lambda e, b=b: e.activation(
                out=beta_tok[:, b * TPB * H:(b + 1) * TPB * H], in_=psum[2][:, 0:TPB * H],
                func=AF.Sigmoid), deps=[mmb], tag=f"bsig{b}")
            bankR[2] = [bsig]
            bneg = op("vector", lambda e, b=b: e.tensor_scalar(
                out=nbeta_tok[:, b * TPB * H:(b + 1) * TPB * H],
                in0=beta_tok[:, b * TPB * H:(b + 1) * TPB * H],
                scalar1=-1.0, scalar2=None, op0=ALU.mult), deps=[bsig], tag=f"bneg{b}")
            bsigs = [bneg]
            KB = {0: [3, 5], 1: [6, 7]}
            vbanks = [0, 1][:NKV]
            sq2 = [sq_scr, sq_scr2]
            mmk_by_ti = {}

            def kv_mms(ti):
                tile = b * TPB + ti
                kbanks = KB[ti % 2][:NKV]
                mmk_last = mmv_last = None
                extra = list(prev_s_adds) if ti % 2 == 1 else []
                for dc in range(NDC):
                    for nh in range(NKV):
                        mmk_last = op("tensor", lambda e, dc=dc, nh=nh, ti=ti, kb=kbanks[nh]: e.matmul(
                            psum[kb][:, 0:WKV],
                            hf_t[:, dc, ti * 128:(ti + 1) * 128],
                            wk_t[:, dc, nh * WKV:(nh + 1) * WKV],
                            start=(dc == 0), stop=(dc == NDC - 1)),
                            deps=(dep0 + bankR[kbanks[nh]] + extra) if dc == 0 else [],
                            tag=f"k{tile}_{dc}_{nh}")
                        mmv_last = op("tensor", lambda e, dc=dc, nh=nh, ti=ti: e.matmul(
                            psum[vbanks[nh]][:, 0:WKV],
                            hf_t[:, dc, ti * 128:(ti + 1) * 128],
                            wv_t[:, dc, nh * WKV:(nh + 1) * WKV],
                            start=(dc == 0), stop=(dc == NDC - 1)),
                            deps=(dep0 + bankR[vbanks[nh]]) if dc == 0 else [],
                            tag=f"v{tile}_{dc}_{nh}")
                mmk_by_ti[ti] = mmk_last
                vevs = []
                for nh in range(NKV):
                    vevs.append(op("vector", lambda e, nh=nh, ti=ti: e.tensor_copy(
                        vtok_blk[:, ti * DI + nh * WKV: ti * DI + (nh + 1) * WKV],
                        psum[vbanks[nh]][:, 0:WKV]), deps=[mmv_last], tag=f"vev{tile}_{nh}"))
                    bankR[vbanks[nh]] = [vevs[-1]]
                kv_done.extend(vevs)

            def knorm_chain(ti):
                tile = b * TPB + ti
                par = ti % 2
                kbanks = KB[par][:NKV]
                ngrp = WKV // 64
                scr = sq2[par]
                rb = rn_sb[:, par * H:(par + 1) * H]
                rdl = None
                for nh in range(NKV):
                    sq = op("scalar", lambda e, nh=nh, par=par, scr=scr: e.activation(
                        out=scr[:, 0:WKV], in_=psum[kbanks[nh]][:, 0:WKV], func=AF.Square),
                        deps=[mmk_by_ti[ti], sqread[par]], tag=f"ksq{tile}_{nh}")
                    rdl = op("vector", lambda e, nh=nh, ngrp=ngrp, scr=scr, rb=rb: e.tensor_reduce(
                        out=rb[:, nh * ngrp:(nh + 1) * ngrp],
                        in_=scr[:, 0:WKV].rearrange("p (g f) -> p g f", f=64),
                        axis=AX.X, op=ALU.add), deps=[sq], tag=f"krd{tile}_{nh}")
                    sqread[par] = rdl
                    yield
                r1 = op("vector", lambda e, rb=rb: e.tensor_scalar(
                    out=rb[:, 0:H], in0=rb[:, 0:H], scalar1=1e-24, scalar2=None,
                    op0=ALU.max), deps=[rdl], tag=f"kmx{tile}")
                yield
                r2 = op("scalar", lambda e, rb=rb: e.activation(
                    out=rb[:, 0:H], in_=rb[:, 0:H], func=AF.Ln), deps=[r1], tag=f"kln{tile}")
                yield
                r3 = op("scalar", lambda e, rb=rb: e.activation(
                    out=rb[:, 0:H], in_=rb[:, 0:H], func=AF.Exp, scale=-0.5),
                    deps=[r2], tag=f"kexp{tile}")
                yield
                kevs = []
                for h in range(H):
                    nh = (h * 64) // WKV
                    bi = kbanks[nh]
                    off = (h * 64) % WKV
                    if nh == 0:
                        kevs.append(op("vector", lambda e, h=h, bi=bi, off=off, ti=ti, rb=rb: e.tensor_scalar(
                            out=ktok_blk[:, ti * DI + h * 64: ti * DI + (h + 1) * 64],
                            in0=psum[bi][:, off:off + 64], scalar1=rb[:, h:h + 1],
                            scalar2=None, op0=ALU.mult), deps=[r3], tag=f"kev{tile}_{h}"))
                    else:
                        kevs.append(op("scalar", lambda e, h=h, bi=bi, off=off, ti=ti, rb=rb: e.activation(
                            out=ktok_blk[:, ti * DI + h * 64: ti * DI + (h + 1) * 64],
                            in_=psum[bi][:, off:off + 64], func=AF.Identity,
                            scale=rb[:, h:h + 1]), deps=[r3], tag=f"kev{tile}_{h}"))
                    if h % 2 == 1:
                        yield
                for nh in range(NKV):
                    bankR[kbanks[nh]] = [kevs[-1]]
                pT = 4 if par == 0 else 2
                p4bf = psum[pT].ap().bitcast(BF16)
                trs_k = []
                for dc in range(NDI):
                    trs_k.append(op("tensor", lambda e, dc=dc, ti=ti, p4bf=p4bf: e.matmul(
                        p4bf[:, dc * 128:(dc + 1) * 128],
                        ktok_blk[:, ti * DI + dc * 128: ti * DI + (dc + 1) * 128],
                        id_bf, is_transpose=True, start=(dc == 0), stop=(dc == NDI - 1)),
                        deps=([kevs[2 * dc], kevs[2 * dc + 1], c_id] + prev_scan + bankR[pT])
                        if dc == 0 else [kevs[2 * dc], kevs[2 * dc + 1]],
                        tag=f"kT{tile}_{dc}"))
                yield
                kt_last[0] = op("vector", lambda e, ti=ti, p4bf=p4bf: e.tensor_copy(
                    kf_t[:, :, ti * 128:(ti + 1) * 128], p4bf[:, 0:NDI * 128].rearrange(
                        "p (c w) -> p c w", w=128)),
                    deps=[trs_k[-1]], tag=f"kTe{tile}")
                bankR[pT] = [kt_last[0]]
                kv_done.append(kt_last[0])

            for half in range(TPB // 2):
                kv_mms(2 * half)
                kv_mms(2 * half + 1)
                gens = [knorm_chain(2 * half), knorm_chain(2 * half + 1)]
                while gens:
                    for g_ in list(gens):
                        try:
                            next(g_)
                        except StopIteration:
                            gens.remove(g_)

            kv_last = kv_done[-1]

            BAP = {0: 2, 1: 5}
            BSQ1 = {0: 3, 1: 0}
            BSQ2 = {0: 4, 1: 1}
            BOX = {0: 4, 1: 1}
            SQE1 = {0: "vector", 1: "scalar"}
            SQE2 = {0: "scalar", 1: "vector"}
            NPAIR = H // 2
            for cb in range(TPB):
                c = b * TPB + cb
                if c > 0:
                    s_evac = op("vector", lambda e: e.tensor_copy(
                        sf_bf[:, 0:(H // 2) * 64], s32_sb[:, 0:(H // 2) * 64]),
                        deps=prev_s_adds, tag=f"scast{c}")
                else:
                    s_evac = None
                quad_prev = list(last_dS_pair)
                dS_ops = []
                NQ = (NPAIR + 1) // 2
                holder = {"l0": [], "ue0": []}

                def make_pre(q):
                    pairs_q = [2 * q + j for j in range(2) if 2 * q + j < NPAIR]
                    st_pre = {"ctx": {}, "partial": {}}

                    def early():
                        for par, p in enumerate(pairs_q):
                            h0, h1 = 2 * p, 2 * p + 1
                            ccol = cb * 128
                            kf_pair = kf_t[:, p, ccol:ccol + 128]
                            qf_pair = qf_t[:, p, ccol:ccol + 128]
                            v_pair = vtok_blk[:, cb * DI + h0 * 64: cb * DI + h0 * 64 + 128]
                            k_pair = ktok_blk[:, cb * DI + h0 * 64: cb * DI + h0 * 64 + 128]
                            sf_p = [sf_bf[0:64, p * 64:(p + 1) * 64],
                                    sf_bf[64:128, p * 64:(p + 1) * 64]]
                            nb = [nbeta_tok[:, c * H + h0: c * H + h0 + 1],
                                  nbeta_tok[:, c * H + h1: c * H + h1 + 1]]
                            bc = [beta_tok[:, c * H + h0: c * H + h0 + 1],
                                  beta_tok[:, c * H + h1: c * H + h1 + 1]]
                            hstart = [kv_last, kt_last[0], q_evacs[-1], bsigs[-1]]
                            b_done = [None, None]
                            for hh in range(2):
                                rows = slice(64 * hh, 64 * hh + 64)
                                bk = psum[hh]
                                op("tensor", lambda e, kf_pair=kf_pair, rows=rows, bk=bk: e.matmul(
                                    bk[:, 0:128], kf_pair[rows, :], kf_pair[rows, :],
                                    start=True, stop=False),
                                    deps=hstart + bankR[hh], tag=f"A{c}_{p}{hh}")
                                op("tensor", lambda e, kf_pair=kf_pair, qf_pair=qf_pair,
                                   rows=rows, bk=bk, c=c: e.matmul(
                                    bk[:, 128:256], kf_pair[rows, :], qf_pair[rows, :],
                                    start=False, stop=(c == 0)), deps=[], tag=f"G{c}_{p}{hh}")
                                if c > 0:
                                    op("tensor", lambda e, kf_pair=kf_pair, rows=rows, bk=bk,
                                       sf0=sf_p[hh]: e.matmul(
                                        bk[:, 256:320], kf_pair[rows, :], sf0,
                                        start=False, stop=False), deps=[s_evac], tag=f"P{c}_{p}{hh}")
                                    b_done[hh] = op("tensor", lambda e, qf_pair=qf_pair, rows=rows,
                                                    bk=bk, sf0=sf_p[hh]: e.matmul(
                                        bk[:, 320:384], qf_pair[rows, :], sf0,
                                        start=False, stop=True), deps=[], tag=f"Oi{c}_{p}{hh}")
                                else:
                                    b_done[hh] = P.per_engine["tensor"][-1]
                            evs = {}
                            for hh in range(2):
                                bk = psum[hh]
                                evs[("M", hh)] = op("vector", lambda e, bk=bk, hh=hh, nb=nb, par=par: e.scalar_tensor_tensor(
                                    out=m_pair[par][:, hh * 128:(hh + 1) * 128], in0=bk[:, 0:128],
                                    scalar=nb[hh], in1=maskL, op0=ALU.mult, op1=ALU.mult),
                                    deps=[b_done[hh], c_maskL] + holder["l0"], tag=f"M{c}_{p}{hh}")
                                if c > 0:
                                    evs[("d", hh)] = op("vector", lambda e, bk=bk, hh=hh, v_pair=v_pair, par=par: e.tensor_tensor(
                                        out=d1_pair[par][:, hh * 64:(hh + 1) * 64],
                                        in0=v_pair[:, hh * 64:(hh + 1) * 64], in1=bk[:, 256:320],
                                        op=ALU.subtract), deps=[b_done[hh]], tag=f"d{c}_{p}{hh}")
                                    evs[("R", hh)] = op("vector", lambda e, hh=hh, bc=bc, par=par: e.tensor_scalar(
                                        out=r_pair[par][:, hh * 64:(hh + 1) * 64],
                                        in0=d1_pair[par][:, hh * 64:(hh + 1) * 64], scalar1=bc[hh],
                                        scalar2=None, op0=ALU.mult),
                                        deps=[evs[("d", hh)]] + holder["ue0"],
                                        tag=f"R{c}_{p}{hh}")
                                else:
                                    evs[("R", hh)] = op("vector", lambda e, hh=hh, v_pair=v_pair, bc=bc, par=par: e.tensor_scalar(
                                        out=r_pair[par][:, hh * 64:(hh + 1) * 64],
                                        in0=v_pair[:, hh * 64:(hh + 1) * 64], scalar1=bc[hh],
                                        scalar2=None, op0=ALU.mult),
                                        deps=hstart + holder["ue0"], tag=f"R{c}_{p}{hh}")
                            qp = q % 2
                            for hh in range(2):
                                bk = psum[hh]
                                evs[("G", hh)] = op("vector", lambda e, bk=bk, hh=hh, par=par, qp=qp: e.tensor_tensor(
                                    out=gm2[qp][par][:, hh * 128:(hh + 1) * 128], in0=bk[:, 128:256],
                                    in1=maskUi, op=ALU.mult),
                                    deps=[b_done[hh], c_maskUi] + last_ox[qp], tag=f"Gm{c}_{p}{hh}")
                                if c > 0:
                                    evs[("O", hh)] = op("vector", lambda e, bk=bk, hh=hh, par=par, qp=qp: e.tensor_copy(
                                        oi2q[qp][:, par * 128 + hh * 64: par * 128 + (hh + 1) * 64],
                                        bk[:, 320:384]),
                                        deps=[b_done[hh]] + last_oev[qp], tag=f"Oe{c}_{p}{hh}")
                                bankR[hh] = [evs[kk] for kk in evs if kk[1] == hh and kk[0] in "MGd"] \
                                    + ([evs[("O", hh)]] if c > 0 else [])
                            st_pre["partial"][par] = dict(
                                p=p, h0=h0, b_done=b_done, evs=evs, v_pair=v_pair,
                                k_pair=k_pair)

                    def late():
                        for par in sorted(st_pre["partial"]):
                            Y = st_pre["partial"][par]
                            p, b_done, evs = Y["p"], Y["b_done"], Y["evs"]
                            pbf = psum[BAP[par]].ap().bitcast(BF16)
                            tr0 = op("tensor", lambda e, pbf=pbf, par=par: e.matmul(
                                pbf[:, 0:128], m_pair[par][:, 0:128], id_bf,
                                is_transpose=True, start=True, stop=False),
                                deps=[evs[("M", 0)], c_id] + bankR[BAP[par]], tag=f"tr{c}_{p}0")
                            tr1 = op("tensor", lambda e, pbf=pbf, par=par: e.matmul(
                                pbf[:, 128:256], m_pair[par][:, 128:256], id_bf,
                                is_transpose=True, start=False, stop=True),
                                deps=[evs[("M", 1)]], tag=f"tr{c}_{p}1")
                            mMt = op("vector", lambda e, pbf=pbf, par=par: e.tensor_copy(
                                mt_pair[par], pbf[:, 0:256]), deps=[tr0, tr1], tag=f"Mt_{c}_{p}")
                            bankR[BAP[par]] = [mMt]
                            st_pre["ctx"][par] = dict(
                                p=p, h0=Y["h0"], evs=evs, mMt=mMt, k_pair=Y["k_pair"],
                                cur_sl=[m_pair[par][:, 0:128], m_pair[par][:, 128:256]],
                                curT_sl=[mt_pair[par][:, 0:128], mt_pair[par][:, 128:256]],
                                cur_tok=[evs[("M", 0)], evs[("M", 1)], mMt],
                                u_ap=r_pair[par], u_tok=[evs[("R", 0)], evs[("R", 1)]],
                                oi_ok=[evs[("O", 0)], evs[("O", 1)]] if c > 0 else [],
                            )
                    return early, late, st_pre

                pend = None
                for q in range(NQ):
                    if pend is None:
                        e_, l_, st_pre = make_pre(q)
                        e_()
                    else:
                        e_, l_, st_pre = pend
                        pend = None
                    l_()
                    ctx = st_pre["ctx"]
                    nxt_pre = make_pre(q + 1) if q + 1 < NQ else None
                    nlev = cfg.get("nlev", 7)
                    nosq = cfg.get("nosq", False)
                    ue0_list = []
                    for k in range(nlev):
                        for par in ctx:
                            X = ctx[par]
                            uo = upair[par][k % 2]
                            op("tensor", lambda e, ct=X["curT_sl"][0], u_ap=X["u_ap"], bi=BAP[par]: e.matmul(
                                psum[bi][:, 256:320], ct, u_ap[:, 0:64],
                                start=True, stop=False),
                                deps=X["cur_tok"] + X["u_tok"] + bankR[BAP[par]],
                                tag=f"ap{c}_{X['p']}_{k}0")
                            if par == 0:
                                ap1 = op("tensor", lambda e, ct=X["curT_sl"][1], u_ap=X["u_ap"], bi=BAP[par]: e.matmul(
                                    psum[bi][:, 320:384], ct, u_ap[:, 64:128],
                                    start=False, stop=True), deps=[], tag=f"ap{c}_{X['p']}_{k}1")
                                ue = op("vector", lambda e, uo=uo, u_ap=X["u_ap"], bi=BAP[par]: e.scalar_tensor_tensor(
                                    out=uo, in0=psum[bi][:, 256:384], scalar=1.0, in1=u_ap,
                                    op0=ALU.mult, op1=ALU.add), deps=[ap1] + X["u_tok"],
                                    tag=f"ue{c}_{X['p']}_{k}")
                            else:
                                op("tensor", lambda e, ct=X["curT_sl"][1], u_ap=X["u_ap"], bi=BAP[par]: e.matmul(
                                    psum[bi][:, 320:384], ct, u_ap[:, 64:128],
                                    start=False, stop=False), deps=[], tag=f"ap{c}_{X['p']}_{k}1")
                                op("tensor", lambda e, u_ap=X["u_ap"], bi=BAP[par]: e.matmul(
                                    psum[bi][:, 256:320], id_bf, u_ap[:, 0:64],
                                    start=False, stop=False), deps=[c_id], tag=f"ai{c}_{X['p']}_{k}0")
                                ap1 = op("tensor", lambda e, u_ap=X["u_ap"], bi=BAP[par]: e.matmul(
                                    psum[bi][:, 320:384], id_bf, u_ap[:, 64:128],
                                    start=False, stop=True), deps=[], tag=f"ai{c}_{X['p']}_{k}1")
                                ue = op("scalar", lambda e, uo=uo, bi=BAP[par]: e.activation(
                                    out=uo, in_=psum[bi][:, 256:384], func=AF.Copy),
                                    deps=[ap1] + X["u_tok"], tag=f"ue{c}_{X['p']}_{k}")
                            bankR[BAP[par]] = [ue]
                            X["u_ap"], X["u_tok"] = uo, [ue]
                            if k == 0:
                                ue0_list.append(ue)
                            if k < nlev - 1 and not nosq:
                                nxt = curbuf[par][(k + 1) % 2]
                                do_sq1 = k < nlev - 2
                                b1, b2 = BSQ1[par], BSQ2[par]
                                sq_evs = []
                                if do_sq1:
                                    op("tensor", lambda e, ct=X["curT_sl"][0], cu=X["cur_sl"][0], b1=b1: e.matmul(
                                        psum[b1][:, 0:128], ct, cu, start=True, stop=False),
                                        deps=X["cur_tok"] + bankR[b1], tag=f"sqa{c}_{X['p']}_{k}0")
                                    sq1_last = op("tensor", lambda e, ct=X["curT_sl"][1], cu=X["cur_sl"][1], b1=b1: e.matmul(
                                        psum[b1][:, 128:256], ct, cu, start=False, stop=True),
                                        deps=[], tag=f"sqa{c}_{X['p']}_{k}1")
                                op("tensor", lambda e, ct=X["curT_sl"][0], cu=X["cur_sl"][0], b2=b2: e.matmul(
                                    psum[b2][:, 256:384], cu, ct, start=True, stop=False),
                                    deps=(X["cur_tok"] + bankR[b2]), tag=f"sqb{c}_{X['p']}_{k}0")
                                sq2_last = op("tensor", lambda e, ct=X["curT_sl"][1], cu=X["cur_sl"][1], b2=b2: e.matmul(
                                    psum[b2][:, 384:512], cu, ct, start=False, stop=True),
                                    deps=[], tag=f"sqb{c}_{X['p']}_{k}1")
                                if do_sq1:
                                    if SQE1[par] == "scalar":
                                        sq_ev1 = op("scalar", lambda e, nxt=nxt, b1=b1: e.activation(
                                            out=nxt[:, 0:256], in_=psum[b1][:, 0:256], func=AF.Copy),
                                            deps=[sq1_last], tag=f"sqe{c}_{X['p']}_{k}a")
                                    else:
                                        sq_ev1 = op("vector", lambda e, nxt=nxt, b1=b1: e.tensor_copy(
                                            nxt[:, 0:256], psum[b1][:, 0:256]),
                                            deps=[sq1_last], tag=f"sqe{c}_{X['p']}_{k}a")
                                    bankR[b1] = [sq_ev1]
                                    sq_evs.append(sq_ev1)
                                    X["cur_sl"] = [nxt[:, 0:128], nxt[:, 128:256]]
                                if SQE2[par] == "scalar":
                                    sq_ev2 = op("scalar", lambda e, nxt=nxt, b2=b2: e.activation(
                                        out=nxt[:, 256:512], in_=psum[b2][:, 256:512], func=AF.Copy),
                                        deps=[sq2_last], tag=f"sqe{c}_{X['p']}_{k}b")
                                else:
                                    sq_ev2 = op("vector", lambda e, nxt=nxt, b2=b2: e.tensor_copy(
                                        nxt[:, 256:512], psum[b2][:, 256:512]),
                                        deps=[sq2_last], tag=f"sqe{c}_{X['p']}_{k}b")
                                bankR[b2] = [sq_ev2]
                                sq_evs.append(sq_ev2)
                                X["curT_sl"] = [nxt[:, 256:384], nxt[:, 384:512]]
                                X["cur_tok"] = sq_evs
                        if k == 0:
                            holder["l0"] = [P.per_engine["tensor"][-1]]
                            holder["ue0"] = list(ue0_list)
                        if k == nlev - 3 and nxt_pre is not None:
                            nxt_pre[0]()
                            pend = nxt_pre
                            nxt_pre = None
                    if nxt_pre is not None:
                        nxt_pre[0]()
                        pend = nxt_pre
                        nxt_pre = None
                    quad_prev = []
                    qp_ = q % 2
                    ox_list = []
                    oev_list = []
                    for par in ctx:
                        X = ctx[par]
                        p = X["p"]; h0 = X["h0"]
                        bo = BOX[par]
                        mmOx0 = op("tensor", lambda e, u_ap=X["u_ap"], bo=bo, par=par, qp_=qp_: e.matmul(
                            psum[bo][:, 0:64], gm2[qp_][par][:, 0:128], u_ap[:, 0:64],
                            start=True, stop=False),
                            deps=[X["evs"][("G", 0)], X["evs"][("G", 1)]] + X["u_tok"] + bankR[bo],
                            tag=f"Ox{c}_{p}0")
                        mmOx1 = op("tensor", lambda e, u_ap=X["u_ap"], bo=bo, par=par, qp_=qp_: e.matmul(
                            psum[bo][:, 64:128], gm2[qp_][par][:, 128:256], u_ap[:, 64:128],
                            start=False, stop=True), deps=[], tag=f"Ox{c}_{p}1")
                        if c > 0:
                            o_ev = op("vector", lambda e, h0=h0, bo=bo, par=par, qp_=qp_: e.scalar_tensor_tensor(
                                out=o_chunk[:, h0 * 64: h0 * 64 + 128], in0=psum[bo][:, 0:128],
                                scalar=1.0, in1=oi2q[qp_][:, par * 128:(par + 1) * 128],
                                op0=ALU.mult, op1=ALU.add),
                                deps=[mmOx1] + X["oi_ok"] + prev_o_readers, tag=f"oev{c}_{p}")
                        else:
                            o_ev = op("vector", lambda e, h0=h0, bo=bo: e.tensor_copy(
                                o_chunk[:, h0 * 64: h0 * 64 + 128], psum[bo][:, 0:128]),
                                deps=[mmOx1] + prev_o_readers, tag=f"oev{c}_{p}")
                        bankR[bo] = [o_ev]
                        ox_list += [mmOx0, mmOx1]
                        oev_list.append(o_ev)
                        dS0 = op("tensor", lambda e, k_pair=X["k_pair"], u_ap=X["u_ap"], p=p: e.matmul(
                            psum[6][0:64, p * 64:(p + 1) * 64], k_pair[:, 0:64],
                            u_ap[:, 0:64], start=(p == 0), stop=(p == NPAIR - 1),
                            tile_position=(0, 0)),
                            deps=X["u_tok"] + [s_evac, o_ev, prev_s_adds[0]]
                            + (bankR[6] if p == 0 else []), tag=f"dS{c}_{p}0")
                        dS1 = op("tensor", lambda e, k_pair=X["k_pair"], u_ap=X["u_ap"], p=p: e.matmul(
                            psum[7][64:128, p * 64:(p + 1) * 64], k_pair[:, 64:128],
                            u_ap[:, 64:128], start=(p == 0), stop=(p == NPAIR - 1),
                            tile_position=(0, 64)),
                            deps=[prev_s_adds[1]] + (bankR[7] if p == 0 else []),
                            tag=f"dS{c}_{p}1")
                        dS_ops += [dS0, dS1]
                        quad_prev += [dS0, dS1]
                    last_ox[qp_] = ox_list + oev_list
                    last_oev[qp_] = oev_list
                last_dS_pair = quad_prev[-2:]
                parts = cfg.get("scanparts", "pxdw")
                last_dS = dS_ops[-1]
                parts = cfg.get("scanparts", "pxdw")
                if "d" not in parts:
                    res_toks[c] = [last_dS, last_dS]
                    continue
                sa1 = op("vector", lambda e: e.tensor_tensor(
                    out=s32_sb[0:64, 0:(H // 2) * 64], in0=psum[6][0:64, 0:(H // 2) * 64],
                    in1=s32_sb[0:64, 0:(H // 2) * 64], op=ALU.add),
                    deps=[dS_ops[-2], dS_ops[-1]], tag=f"saddA{c}")
                sa2 = op("vector", lambda e: e.tensor_tensor(
                    out=s32_sb[64:128, 0:(H // 2) * 64], in0=psum[7][64:128, 0:(H // 2) * 64],
                    in1=s32_sb[64:128, 0:(H // 2) * 64], op=ALU.add),
                    deps=[dS_ops[-2], dS_ops[-1]], tag=f"saddB{c}")
                prev_s_adds = [sa1, sa2]
                if "w" not in parts:
                    res_toks[c] = [sa1, sa2]
                    continue
                p2bf = psum[2].ap().bitcast(BF16)
                trs_o = []
                for dc in range(NDI):
                    trs_o.append(op("tensor", lambda e, dc=dc, p2bf=p2bf: e.matmul(
                        p2bf[:, dc * 128:(dc + 1) * 128],
                        o_chunk[:, dc * 128:(dc + 1) * 128],
                        id_bf, is_transpose=True, start=(dc == 0), stop=(dc == NDI - 1)),
                        deps=([last_dS, c_id] + prev_of_readers + bankR[2])
                        if dc == 0 else [], tag=f"oT{c}_{dc}"))
                of_ops = [op("vector", lambda e, p2bf=p2bf: e.tensor_copy(
                    of_chunk[:, 0:NDI * 128], p2bf[:, 0:NDI * 128]),
                    deps=[trs_o[-1]], tag=f"oTe{c}")]
                bankR[2] = [of_ops[-1]]
                prev_o_readers = list(of_ops)
                if dbg:
                    dmp = op("gpsimd", lambda e, c=c: e.dma_start(
                        out=dbg_o[c], in_=o_chunk), deps=[last_dS], dma=True,
                        group=f"dbgo{c}", tag=f"dbgo{c}")
                    prev_o_readers = prev_o_readers + [dmp]
                res_c = []
                mmw_all = []
                for n2 in range(2):
                    bi = (3, 5)[n2]
                    mmw = None
                    for dc in range(NDI):
                        mmw = op("tensor", lambda e, dc=dc, n2=n2, bi=bi: e.matmul(
                            psum[bi][:, 0:512], of_chunk[:, dc * 128:(dc + 1) * 128],
                            wout_t[:, dc, n2 * 512:n2 * 512 + 512],
                            start=(dc == 0), stop=(dc == NDI - 1)),
                            deps=(of_ops + bankR[bi]) if dc == 0 else [],
                            tag=f"wo{c}_{n2}_{dc}")
                    mmw_all.append(mmw)
                    res_c.append(op("vector", lambda e, n2=n2, c=c, bi=bi: e.tensor_tensor(
                        out=x_sb[:, c * 1024 + n2 * 512: c * 1024 + n2 * 512 + 512],
                        in0=psum[bi][:, 0:512],
                        in1=x_sb[:, c * 1024 + n2 * 512: c * 1024 + n2 * 512 + 512],
                        op=ALU.add), deps=[mmw], tag=f"res{c}_{n2}"))
                    bankR[bi] = [res_c[-1]]
                prev_of_readers = mmw_all
                res_toks[c] = res_c
            prev_scan_join = ([last_dS] + res_toks[b * TPB + TPB - 1]
                              if mode != "proj" else [kv_last, q_evacs[-1]])

        phase_a_last = [P.per_engine[e][-1] for e in ENGINES if P.per_engine[e]]
        barrier = phase_a_last
        if mode != "full":
            outr0 = out_d.rearrange("(n p) d -> p n d", p=128)
            sA = []
            for b2 in range(NB):
                sA.append(op("sync", lambda e, b2=b2: e.dma_start(
                    out=outr0[:, b2 * TPB:(b2 + 1) * TPB, :],
                    in_=x_sb[:, b2 * TPB * 1024:(b2 + 1) * TPB * 1024].rearrange(
                        "p (n d) -> p n d", d=1024)),
                    deps=barrier, dma=True, group="store", tag=f"stA{b2}"))
            op("sync", lambda e: e.nop(), deps=sA, tag="final_joinA")
            P.emit(st)
            return nc
        a16.reset(mark16)
        CAP = cfg.get("cap", 512)
        SLT = CAP // 128
        gbuf = [a16.alloc(NDC * F) for _ in range(2)]
        ubuf = [a16.alloc(NDC * F)]
        wdbuf = [a16.alloc(NFC * D) for _ in range(2)]
        h2_stage = a16.alloc(1024)
        hftile = a16.alloc(1024)
        bd_row = [h2_stage, hftile]
        hfe = [a16.alloc(NDC * CAP) for _ in range(2)]
        gated = a16.alloc(NFC * CAP)
        g1_scr = [a16.alloc(CAP) for _ in range(2)]
        s_scr = [a16.alloc(CAP) for _ in range(2)]
        u1_scr = [a16.alloc(CAP) for _ in range(2)]
        hblk = [a16.alloc(1024) for _ in range(CAP // 128)]
        gfin_bufs = hblk[0:2]
        outst = [a16.alloc(1024) for _ in range(2)]
        gfin = gfin_bufs
        ge_all = a16.alloc(NT * E)
        onesF_bf = a16.alloc(128)
        onesU_bf = a16.alloc(128)
        em_all = a32.alloc(NT * 2 * E)
        iotaE2 = a32.alloc(2 * E)
        iotaE = a32.alloc(E)
        cwsel = a32.alloc(2 * NT)
        hfe_t = [h_.rearrange("p (c w) -> p c w", w=CAP) for h_ in hfe]
        gbuf_t = [g.rearrange("p (c f) -> p c f", f=F) for g in gbuf]
        ubuf_t = [u_.rearrange("p (c f) -> p c f", f=F) for u_ in ubuf] * 2
        wdbuf_t = [w_.rearrange("p (c f) -> p c f", f=D) for w_ in wdbuf]
        tid_i32 = st.enter_context(nc.sbuf_tensor("tid_i32", [128, NT], I32))
        dst_i32 = st.enter_context(nc.sbuf_tensor("dst_i32", [128, 2 * NT], I32))
        eix_i32 = st.enter_context(nc.sbuf_tensor("eix_i32", [128, 2 * SLT], I32))
        pre_i32 = st.enter_context(nc.sbuf_tensor("pre_i32", [128, E * CAP // 128], I32))

        c_onesF = op("gpsimd", lambda e: e.memset(onesF_bf, 1.0), deps=barrier, tag="onesF")
        c_onesU = op("vector", lambda e: e.tensor_scalar(
            out=onesU_bf, in0=iota_f, scalar1=iota_p[:, 0:1], scalar2=None,
            op0=ALU.is_gt), deps=[c_iota_f, c_iota_p] + barrier, tag="onesU")
        c_iotaE = op("gpsimd", lambda e: e.iota(iotaE, [[1, E]], channel_multiplier=0,
                     allow_small_or_imprecise_dtypes=True), deps=barrier, tag="iotaE")
        c_iotaE2 = op("gpsimd", lambda e: e.iota(iotaE2, [[0, 2], [1, E]],
                      channel_multiplier=0, allow_small_or_imprecise_dtypes=True),
                      deps=barrier, tag="iotaE2")
        c_tid = op("gpsimd", lambda e: e.iota(tid_i32[:, :], [[128, NT]],
                   channel_multiplier=1), deps=barrier, tag="tid")
        c_pre = op("gpsimd", lambda e: e.iota(pre_i32[:, :], [[0, E * CAP // 128]],
                   base=0, channel_multiplier=0), deps=barrier, tag="pre")
        pre_st = op("sync", lambda e: e.dma_start(
            out=idl_d.rearrange("(p b) -> p b", p=128), in_=pre_i32[:, :]),
            deps=[c_pre] + barrier, dma=True, group="idpre", tag="prest")

        pre_w = {}
        for e_i in range(min(2, E)):
            sl = e_i
            dw = {}
            dw["ld_g"] = op("gpsimd", lambda e, e_i=e_i, sl=sl: e.dma_start(
                out=gbuf_t[sl], in_=wgate_d[e_i]),
                deps=barrier, dma=True, group=f"wgE{sl}", tag=f"ldg{e_i}")
            dw["ld_d"] = op("gpsimd", lambda e, e_i=e_i, sl=sl: e.dma_start(
                out=wdbuf_t[sl], in_=wd_d[e_i]),
                deps=barrier, dma=True, group=f"wdE{sl}", tag=f"ldd{e_i}")
            dw["ld_bg1"] = op("sync", lambda e, e_i=e_i, sl=sl: e.dma_start(
                out=bgug_col[sl][:, :], in_=bgug_d[e_i]),
                deps=barrier, dma=True, group=f"bgEg{sl}", tag=f"ldbgg{e_i}")
            dw["ld_bg2"] = op("sync", lambda e, e_i=e_i, sl=sl: e.dma_start(
                out=bguu_col[sl][:, :], in_=bguu_d[e_i]),
                deps=barrier, dma=True, group=f"bgEu{sl}", tag=f"ldbgu{e_i}")
            if e_i == 0:
                dw["ld_u"] = op("gpsimd", lambda e: e.dma_start(
                    out=ubuf_t[0], in_=wup_d[0]),
                    deps=barrier, dma=True, group="wuE", tag="ldu0")
            pre_w[e_i] = dw

        rc = a32.alloc(40)
        lgt2 = [lgt_sb, a32.alloc(E)]
        junk2 = [a32.alloc(2 * E), a32.alloc(2 * E)]
        h2buf = [h2_stage, gated[:, 1024:2048]]
        hfbuf = [hftile, gated[:, 2048:3072]]
        dumpb = [gated[:, 0:1024], gated[:, 3072:4096]]
        h2_ready = []
        id_scat = []
        cw_ready = {}
        tq3_all = {}
        prev_trs = {0: [], 1: []}
        prev_mml = {0: [], 1: []}
        prev_scrR = {0: [], 1: []}
        prev_h2s = {0: [], 1: []}

        def route_chain(tile):
            par = tile % 2
            src_t = x_sb[:, tile * 1024:(tile + 1) * 1024]
            dst = h2buf[par]
            dump = dumpb[par]

            def col(i, w=1):
                return rc[:, par * 20 + i:par * 20 + i + w]
            ssum, ssq, mean, vpe, rstd, mrstd = [col(i) for i in range(6)]
            m1, m2, em2, dinv, nm1 = col(6), col(7), col(8), col(9), col(10)
            e12 = col(11, 2)
            pos12 = col(13, 2)
            ogc2 = col(15, 2)
            dstf2 = col(17, 2)
            scrE = lgt2[par][:, 0:E]
            junk = junk2[par]
            cwt = cw_sb[:, tile * E:(tile + 1) * E]
            emt = em_all[:, tile * 2 * E:(tile + 1) * 2 * E]
            eqm = emt[:, 0:E]
            m2m = emt[:, E:2 * E]
            gea = ge_all[:, tile * E:(tile + 1) * E]
            deps0 = barrier + res_toks[tile] + prev_h2s[par] + prev_trs[par]
            t1 = op("vector", lambda e: e.tensor_reduce(
                out=ssum, in_=src_t, axis=AX.X, op=ALU.add), deps=deps0, tag=f"r{tile}_sum")
            yield
            t2 = op("scalar", lambda e: e.activation(
                out=dump, in_=src_t, func=AF.Square, accum_out=ssq),
                deps=deps0, tag=f"r{tile}_sq")
            yield
            t3 = op("vector", lambda e: e.tensor_scalar(
                out=mean, in0=ssum, scalar1=1.0 / 1024, scalar2=None, op0=ALU.mult),
                deps=[t1], tag=f"r{tile}_mean")
            yield
            t4 = op("vector", lambda e: e.tensor_scalar(
                out=vpe, in0=mean, scalar1=mean, scalar2=None, op0=ALU.mult),
                deps=[t3], tag=f"r{tile}_m2")
            yield
            t5 = op("vector", lambda e: e.scalar_tensor_tensor(
                out=vpe, in0=ssq, scalar=1.0 / 1024, in1=vpe,
                op0=ALU.mult, op1=ALU.subtract), deps=[t2, t4], tag=f"r{tile}_var")
            yield
            t6 = op("scalar", lambda e: e.activation(
                out=rstd, in_=vpe, func=AF.Ln, bias=eps_col),
                deps=[t5, c_eps], tag=f"r{tile}_ln")
            yield
            t7 = op("scalar", lambda e: e.activation(
                out=rstd, in_=rstd, func=AF.Exp, scale=-0.5), deps=[t6], tag=f"r{tile}_exp")
            yield
            t8 = op("vector", lambda e: e.tensor_scalar(
                out=mrstd, in0=mean, scalar1=rstd, scalar2=None, op0=ALU.mult),
                deps=[t7, t3], tag=f"r{tile}_mr")
            yield
            t9 = op("vector", lambda e: e.tensor_scalar(
                out=dst, in0=src_t, scalar1=rstd, scalar2=mrstd,
                op0=ALU.mult, op1=ALU.subtract), deps=[t8, t7] + deps0, tag=f"r{tile}_nrm")
            yield
            h2s = op("sync", lambda e, tile=tile: e.dma_start(
                out=h2d_d[tile * 128:(tile + 1) * 128, :], in_=dst),
                deps=[t9], dma=True, group=f"h2s{par}", tag=f"h2s{tile}")
            h2_ready.append(h2s)
            prev_h2s[par] = [h2s]
            yield
            pT = 6 + par
            pbf = psum[pT].ap().bitcast(BF16)
            trs = []
            for dc in range(NDC):
                trs.append(op("tensor", lambda e, dc=dc, pbf=pbf: e.matmul(
                    pbf[:, dc * 128:(dc + 1) * 128], dst[:, dc * 128:(dc + 1) * 128],
                    id_bf, is_transpose=True, start=(dc == 0), stop=(dc == NDC - 1)),
                    deps=([t9, c_id] + bankR[pT]) if dc == 0 else [],
                    tag=f"h2T{tile}_{dc}"))
            prev_trs[par] = [trs[-1]]
            yield
            hfev = op("vector", lambda e, pbf=pbf: e.tensor_copy(
                hfbuf[par], pbf[:, 0:NDC * 128]), deps=[trs[-1]] + prev_mml[par],
                tag=f"h2Te{tile}")
            bankR[pT] = [hfev]
            yield
            lgb = 4 + par
            mml = None
            for dc in range(NDC):
                mml = op("tensor", lambda e, dc=dc, lgb=lgb: e.matmul(
                    psum[lgb][:, 0:E], hfbuf[par][:, dc * 128:(dc + 1) * 128],
                    wgt_t[:, dc, :], start=(dc == 0), stop=False),
                    deps=([hfev] + bankR[lgb]) if dc == 0 else [],
                    tag=f"lg{tile}_{dc}")
            mml = op("tensor", lambda e, lgb=lgb: e.matmul(
                psum[lgb][:, 0:E], ones1[0:1, :], bg_row[0:1, 0:E], start=False, stop=True),
                deps=[c_ones], tag=f"lgb{tile}")
            prev_mml[par] = [mml]
            yield
            lcp = op("vector", lambda e, lgb=lgb: e.tensor_copy(scrE, psum[lgb][:, 0:E]),
                     deps=[mml] + prev_scrR[par], tag=f"lcp{tile}")
            bankR[lgb] = [lcp]
            yield
            t1m = op("vector", lambda e: e.tensor_reduce(
                out=m1, in_=scrE, axis=AX.X, op=ALU.max), deps=[lcp], tag=f"m1_{tile}")
            yield
            t2m = op("vector", lambda e, cwt=cwt: e.tensor_scalar(
                out=cwt, in0=scrE, scalar1=m1, scalar2=-1e30,
                op0=ALU.is_equal, op1=ALU.mult), deps=[t1m], tag=f"eq_{tile}")
            yield
            t3m = op("vector", lambda e, cwt=cwt: e.tensor_tensor(
                out=cwt, in0=cwt, in1=scrE, op=ALU.add), deps=[t2m], tag=f"lm_{tile}")
            yield
            t4m = op("vector", lambda e, cwt=cwt: e.tensor_reduce(
                out=m2, in_=cwt, axis=AX.X, op=ALU.max), deps=[t3m], tag=f"m2_{tile}")
            yield
            t5m = op("vector", lambda e: e.tensor_scalar(
                out=nm1, in0=m1, scalar1=-1.0, scalar2=None, op0=ALU.mult),
                deps=[t1m], tag=f"nm1_{tile}")
            yield
            t6m = op("scalar", lambda e: e.activation(
                out=em2, in_=m2, func=AF.Exp, bias=nm1), deps=[t4m, t5m], tag=f"em2_{tile}")
            yield
            t7m = op("vector", lambda e: e.tensor_scalar(
                out=dinv, in0=em2, scalar1=1.0, scalar2=None, op0=ALU.add),
                deps=[t6m], tag=f"den_{tile}")
            yield
            t8m = op("vector", lambda e, tile=tile: e.reciprocal(
                cwsel[:, 2 * tile:2 * tile + 1], dinv), deps=[t7m], tag=f"rec_{tile}")
            yield
            c2c = op("vector", lambda e, tile=tile: e.tensor_scalar(
                out=cwsel[:, 2 * tile + 1:2 * tile + 2],
                in0=cwsel[:, 2 * tile:2 * tile + 1], scalar1=-1.0, scalar2=1.0,
                op0=ALU.mult, op1=ALU.add), deps=[t8m], tag=f"cw2_{tile}")
            cw_ready[tile] = c2c
            yield
            tq1 = op("vector", lambda e, eqm=eqm: e.tensor_scalar(
                out=eqm, in0=scrE, scalar1=m1, scalar2=None, op0=ALU.is_equal),
                deps=[t1m, t4m], tag=f"eqm_{tile}")
            yield
            t10 = op("vector", lambda e, cwt=cwt: e.tensor_scalar(
                out=cwt, in0=scrE, scalar1=m2, scalar2=None, op0=ALU.is_ge),
                deps=[t4m], tag=f"ge_{tile}")
            yield
            tq2 = op("vector", lambda e, m2m=m2m, cwt=cwt, eqm=eqm: e.tensor_tensor(
                out=m2m, in0=cwt, in1=eqm, op=ALU.subtract),
                deps=[t10, tq1], tag=f"m2m_{tile}")
            yield
            tq3 = op("vector", lambda e, gea=gea, cwt=cwt: e.tensor_scalar(
                out=gea, in0=cwt, scalar1=1.0, scalar2=None, op0=ALU.mult),
                deps=[t10], tag=f"geb_{tile}")
            tq3_all[tile] = tq3
            yield
            tq45a = op("vector", lambda e, emt=emt: e.tensor_tensor(
                out=junk, in0=emt, in1=iotaE2, op=ALU.mult),
                deps=[tq1, tq2, c_iotaE2], tag=f"e12a_{tile}")
            prev_scrR[par] = [t10]
            yield
            tq45 = op("vector", lambda e: e.tensor_reduce(
                out=e12, in_=junk.rearrange("p (g f) -> p g f", f=E),
                axis=AX.X, op=ALU.add), deps=[tq45a], tag=f"e12_{tile}")
            yield
            pB = 2 + par
            pdep = [c_onesF, c_onesU, tq3] + bankR[pB]
            if tile >= 1:
                pdep.append(tq3_all[tile - 1])
            pmm = None
            first = True
            for j in range(tile):
                for hf in range(2):
                    pmm = op("tensor", lambda e, j=j, pB=pB, hf=hf, st_=first: e.matmul(
                        psum[pB][:, hf * E:(hf + 1) * E], onesF_bf,
                        ge_all[:, j * E:(j + 1) * E],
                        start=st_, stop=False),
                        deps=pdep if first else [],
                        tag=f"pm{tile}_{j}_{hf}")
                    first = False
            for hf in range(2):
                pmm = op("tensor", lambda e, tile=tile, pB=pB, hf=hf, st_=first: e.matmul(
                    psum[pB][:, hf * E:(hf + 1) * E], onesU_bf,
                    ge_all[:, tile * E:(tile + 1) * E],
                    start=st_, stop=(hf == 1)),
                    deps=pdep if first else [],
                    tag=f"pmU{tile}_{hf}")
                first = False
            yield
            tp12a = op("vector", lambda e, emt=emt, pB=pB: e.tensor_tensor(
                out=junk, in0=emt, in1=psum[pB][:, 0:2 * E], op=ALU.mult),
                deps=[pmm, tq45], tag=f"p12a_{tile}")
            yield
            tp12 = op("vector", lambda e: e.tensor_reduce(
                out=pos12, in_=junk.rearrange("p (g f) -> p g f", f=E),
                axis=AX.X, op=ALU.add), deps=[tp12a], tag=f"p12_{tile}")
            bankR[pB] = [tp12]
            yield
            dg2 = op("vector", lambda e: e.tensor_scalar(
                out=ogc2, in0=pos12, scalar1=float(CAP - 1), scalar2=None,
                op0=ALU.min), deps=[tp12], tag=f"og{tile}")
            yield
            dg1 = op("vector", lambda e: e.scalar_tensor_tensor(
                out=dstf2, in0=e12, scalar=float(CAP), in1=ogc2,
                op0=ALU.mult, op1=ALU.add), deps=[dg2, tq45], tag=f"dd{tile}")
            yield
            dg4 = op("vector", lambda e, tile=tile: e.tensor_copy(
                dst_i32[:, 2 * tile:2 * tile + 2], dstf2),
                deps=[dg1], tag=f"di{tile}")
            yield
            for slot in range(2):
                id_scat.append(op("gpsimd", lambda e, tile=tile, slot=slot: e.indirect_dma_start(
                    out=idl_d.rearrange("(a b) -> a b", b=1),
                    out_offset=IndirectOffsetOnAxis(ap=dst_i32[:, 2 * tile + slot:2 * tile + slot + 1], axis=0),
                    in_=tid_i32[:, tile:tile + 1], in_offset=None),
                    deps=[dg4, c_tid, pre_st], dma=True, group="ids", tag=f"sc{tile}_{slot}"))
                yield

        chains = [route_chain(0)]
        nxt_tile = 1
        if NT > 1:
            chains.append(route_chain(1))
            nxt_tile = 2
        while chains:
            for g_ in list(chains):
                try:
                    next(g_)
                except StopIteration:
                    chains.remove(g_)
                    if nxt_tile < NT:
                        chains.append(route_chain(nxt_tile))
                        nxt_tile += 1

        for e_i in range(min(2, E)):
            bdep = h2_ready[-2:] if e_i == 0 else prev_mml[0] + prev_trs[0]
            pre_w[e_i]["ld_bd"] = op("gpsimd", lambda e, e_i=e_i: e.dma_start(
                out=bd_row[e_i][0:1, :], in_=bd_d[e_i].rearrange("(a b) -> a b", a=1)),
                deps=bdep, dma=True, group=f"bdE{e_i}", tag=f"ldbd{e_i}")

        last_g_rd = [None, None]
        last_u_rd = [None, None]
        last_wd_rd = [None, None]
        last_bd_rd = [None, None]
        last_bg_rd = [None, None]
        last_eix_rd = [None, None]
        last_hfe_rd = [None, None]
        last_hblk_rd = [None] * SLT
        ids_last = id_scat[-1]
        h2s_last = h2_ready[-2:]
        oed_stores = []
        outst_rd = [[], []]
        last_down_mm = None
        prevs = {0: (None, None, None), 1: (None, None, None)}
        hfe_ready = [None, None]

        def prefetch_dma(e_i):
            sl = e_i % 2
            if e_i in pre_w:
                d = pre_w[e_i]
            else:
                d = {}
                d["ld_g"] = op("gpsimd", lambda e, e_i=e_i, sl=sl: e.dma_start(
                    out=gbuf_t[sl], in_=wgate_d[e_i]),
                    deps=[last_g_rd[sl]], dma=True, group=f"wgE{sl}", tag=f"ldg{e_i}")
                d["ld_d"] = op("gpsimd", lambda e, e_i=e_i, sl=sl: e.dma_start(
                    out=wdbuf_t[sl], in_=wd_d[e_i]),
                    deps=[last_wd_rd[sl]], dma=True, group=f"wdE{sl}", tag=f"ldd{e_i}")
                d["ld_bd"] = op("gpsimd", lambda e, e_i=e_i, sl=sl: e.dma_start(
                    out=bd_row[sl][0:1, :], in_=bd_d[e_i].rearrange("(a b) -> a b", a=1)),
                    deps=[last_bd_rd[sl]], dma=True, group=f"bdE{sl}", tag=f"ldbd{e_i}")
                d["ld_bg1"] = op("sync", lambda e, e_i=e_i, sl=sl: e.dma_start(
                    out=bgug_col[sl][:, :], in_=bgug_d[e_i]),
                    deps=[last_bg_rd[sl]], dma=True, group=f"bgEg{sl}", tag=f"ldbgg{e_i}")
                d["ld_bg2"] = op("sync", lambda e, e_i=e_i, sl=sl: e.dma_start(
                    out=bguu_col[sl][:, :], in_=bguu_d[e_i]),
                    deps=[last_bg_rd[sl]], dma=True, group=f"bgEu{sl}", tag=f"ldbgu{e_i}")
            ld_ix = op("sync", lambda e, e_i=e_i, sl=sl: e.dma_start(
                out=eix_i32[:, sl * SLT:(sl + 1) * SLT],
                in_=idl_d.rearrange("(ee b p) -> ee p b", p=128, b=SLT)[e_i],
                allow_slow_non_contiguous=True),
                deps=[last_eix_rd[sl], ids_last], dma=True,
                group=f"eix{sl}", tag=f"ldix{e_i}")
            d["gg"] = []
            for blk in range(SLT):
                d["gg"].append(op("gpsimd", lambda e, sl=sl, blk=blk: e.indirect_dma_start(
                    out=hblk[blk], out_offset=None, in_=h2d_d,
                    in_offset=IndirectOffsetOnAxis(
                        ap=eix_i32[:, sl * SLT + blk:sl * SLT + blk + 1], axis=0)),
                    deps=[ld_ix, last_hblk_rd[blk]] + h2s_last,
                    dma=True, group=f"hg{blk}", tag=f"hg{e_i}_{blk}"))
            last_eix_rd[sl] = d["gg"][-1]
            return d

        def prefetch_T_gen(e_i, d, out):
            sl = e_i % 2
            for blk in range(SLT):
                pbi = 6 + (blk % 2)
                pbf = psum[pbi].ap().bitcast(BF16)
                trs_g = []
                for dc in range(NDC):
                    trs_g.append(op("tensor", lambda e, dc=dc, blk=blk, pbf=pbf: e.matmul(
                        pbf[:, dc * 128:(dc + 1) * 128], hblk[blk][:, dc * 128:(dc + 1) * 128],
                        id_bf, is_transpose=True, start=(dc == 0), stop=(dc == NDC - 1)),
                        deps=([d["gg"][blk], c_id] + bankR[pbi]) if dc == 0 else [],
                        tag=f"gT{e_i}_{blk}_{dc}"))
                last_hblk_rd[blk] = trs_g[-1]
                hfe_done = op("vector", lambda e, blk=blk, pbf=pbf, sl=sl: e.tensor_copy(
                    hfe_t[sl][:, :, blk * 128:(blk + 1) * 128], pbf[:, 0:NDC * 128].rearrange(
                        "p (c w) -> p c w", w=128)),
                    deps=[trs_g[-1], last_hfe_rd[sl]], tag=f"gTe{e_i}_{blk}")
                bankR[pbi] = [hfe_done]
                out[0] = hfe_done
                yield

        def prefetch_T(e_i, d):
            out = [None]
            for _ in prefetch_T_gen(e_i, d, out):
                pass
            return out[0]

        def emit_ld_u(e_i, extra):
            return op("gpsimd", lambda e, e_i=e_i: e.dma_start(
                out=ubuf_t[0], in_=wup_d[e_i]),
                deps=[last_u_rd[0]] + extra, dma=True, group="wuE", tag=f"ldu{e_i}")

        pfd = {0: prefetch_dma(0)}
        hfe_ready[0] = prefetch_T(0, pfd[0])
        for e_i in range(E):
            sl = e_i % 2
            d = pfd.pop(e_i)
            if e_i + 1 < E:
                pfd[e_i + 1] = prefetch_dma(e_i + 1)
            ld_g, ld_u, ld_d = d["ld_g"], d["ld_u"], d["ld_d"]
            ld_bd, ld_bg1, ld_bg2 = d["ld_bd"], d["ld_bg1"], d["ld_bg2"]
            gated_done = []
            for j in range(NFC):
                bgi = j % 2
                bui = 2 + (j % 2)
                mmg = mmu = None
                for dc in range(NDC):
                    mmg = op("tensor", lambda e, dc=dc, j=j, sl=sl, bgi=bgi: e.matmul(
                        psum[bgi][:, 0:CAP], gbuf_t[sl][:, dc, j * 128:(j + 1) * 128],
                        hfe_t[sl][:, dc, :], start=(dc == 0), stop=(dc == NDC - 1)),
                        deps=([ld_g, hfe_ready[sl]] + bankR[bgi]) if dc == 0 else [],
                        tag=f"g{e_i}_{j}_{dc}")
                    mmu = op("tensor", lambda e, dc=dc, j=j, bui=bui, sl=sl: e.matmul(
                        psum[bui][:, 0:CAP], ubuf_t[sl][:, dc, j * 128:(j + 1) * 128],
                        hfe_t[sl][:, dc, :], start=(dc == 0), stop=(dc == NDC - 1)),
                        deps=([ld_u] + bankR[bui]) if dc == 0 else [],
                        tag=f"u{e_i}_{j}_{dc}")
                last_g_rd[sl] = mmg
                last_u_rd[0] = mmu
                last_hfe_rd[sl] = mmu
                pj = j % 2
                p2, p4, p5 = prevs[pj]
                e1 = op("vector", lambda e, j=j, sl=sl, bgi=bgi, pj=pj: e.tensor_scalar(
                    out=g1_scr[pj], in0=psum[bgi][:, 0:CAP], scalar1=bgug_col[sl][:, j:j + 1],
                    scalar2=7.0, op0=ALU.add, op1=ALU.min),
                    deps=[mmg, ld_bg1, p2, p4], tag=f"eg{e_i}_{j}")
                bankR[bgi] = [e1]
                e2 = op("scalar", lambda e, pj=pj: e.activation(
                    out=s_scr[pj], in_=g1_scr[pj], func=AF.Sigmoid, scale=1.702),
                    deps=[e1, p4], tag=f"es{e_i}_{j}")
                e3 = op("scalar", lambda e, j=j, sl=sl, bui=bui, pj=pj: e.activation(
                    out=u1_scr[pj], in_=psum[bui][:, 0:CAP], func=AF.Identity,
                    bias=bguu_col[sl][:, j:j + 1]),
                    deps=[mmu, ld_bg2, p5], tag=f"eu{e_i}_{j}")
                bankR[bui] = [e3]
                e4 = op("vector", lambda e, pj=pj: e.tensor_tensor(
                    out=g1_scr[pj], in0=g1_scr[pj], in1=s_scr[pj], op=ALU.mult),
                    deps=[e2, e1], tag=f"eglu{e_i}_{j}")
                e5 = op("vector", lambda e, pj=pj: e.tensor_scalar(
                    out=u1_scr[pj], in0=u1_scr[pj], scalar1=7.0, scalar2=-7.0,
                    op0=ALU.min, op1=ALU.max), deps=[e3], tag=f"ecl{e_i}_{j}")
                e6 = op("vector", lambda e, j=j, pj=pj: e.scalar_tensor_tensor(
                    out=gated[:, j * CAP:(j + 1) * CAP], in0=u1_scr[pj], scalar=1.0,
                    in1=g1_scr[pj], op0=ALU.add, op1=ALU.mult),
                    deps=[e5, e4] + ([last_down_mm] if e_i > 0 and j < 2 else []),
                    tag=f"egd{e_i}_{j}")
                last_bg_rd[sl] = e3
                gated_done.append(e6)
                prevs[pj] = (e2, e4, e6)
            if e_i + 1 < E:
                pfd[e_i + 1]["ld_u"] = emit_ld_u(e_i + 1, [])
            tout = [None]
            tgen = (prefetch_T_gen(e_i + 1, pfd[e_i + 1], tout)
                    if e_i + 1 < E else None)
            for stt_ in range(SLT):
                if tgen is not None:
                    try:
                        next(tgen)
                    except StopIteration:
                        tgen = None
                mmd_h = {}
                for fc in range(NFC):
                    for dh in range(2):
                        bi = 4 + dh
                        mmd_h[dh] = op("tensor", lambda e, fc=fc, stt_=stt_, dh=dh, bi=bi, sl=sl: e.matmul(
                            psum[bi][:, 0:512],
                            gated[:, fc * CAP + stt_ * 128: fc * CAP + stt_ * 128 + 128],
                            wdbuf_t[sl][:, fc, dh * 512:dh * 512 + 512],
                            start=(fc == 0), stop=False),
                            deps=([gated_done[fc], ld_d] + bankR[bi]) if fc == 0
                            else [gated_done[fc]],
                            tag=f"d{e_i}_{stt_}_{dh}_{fc}")
                last_wd_rd[sl] = mmd_h[1]
                evs_d = []
                for dh in range(2):
                    bi = 4 + dh
                    mmdb = op("tensor", lambda e, dh=dh, bi=bi, sl=sl: e.matmul(
                        psum[bi][:, 0:512], ones1[0:1, :],
                        bd_row[sl][0:1, dh * 512:dh * 512 + 512],
                        start=False, stop=True), deps=[ld_bd, c_ones],
                        tag=f"db{e_i}_{stt_}_{dh}")
                    last_bd_rd[sl] = mmdb
                    if dh == 0:
                        evs_d.append(op("vector", lambda e, stt_=stt_, bi=bi: e.tensor_copy(
                            outst[stt_ % 2][:, 0:512], psum[bi][:, 0:512]),
                            deps=[mmdb] + outst_rd[stt_ % 2], tag=f"de{e_i}_{stt_}_0"))
                    else:
                        evs_d.append(op("scalar", lambda e, stt_=stt_, bi=bi: e.activation(
                            out=outst[stt_ % 2][:, 512:1024], in_=psum[bi][:, 0:512],
                            func=AF.Copy), deps=[mmdb] + outst_rd[stt_ % 2],
                            tag=f"de{e_i}_{stt_}_1"))
                    bankR[bi] = [evs_d[-1]]
                st_o = op("sync", lambda e, e_i=e_i, stt_=stt_: e.dma_start(
                    out=oed_d[e_i * CAP + stt_ * 128: e_i * CAP + (stt_ + 1) * 128, :],
                    in_=outst[stt_ % 2]), deps=evs_d, dma=True, group=f"oes{stt_ % 2}",
                    tag=f"oes{e_i}_{stt_}")
                oed_stores.append(st_o)
                outst_rd[stt_ % 2] = [st_o]
                last_down_mm = mmd_h[1]
            if tgen is not None:
                for _ in tgen:
                    pass
            if e_i + 1 < E:
                hfe_ready[(e_i + 1) % 2] = tout[0]

        if dbg:
            dmp1 = op("sync", lambda e: e.dma_start(
                out=dbg_idl.rearrange("(p b) -> p b", p=128),
                in_=idl_d.rearrange("(p b) -> p b", p=128)), deps=id_scat, dma=True,
                group="dbgi", tag="dbgidl")
            op("sync", lambda e: e.dma_start(out=dbg_dst, in_=dst_i32[:, :]),
               deps=[d for (t, d) in [] ] + id_scat, dma=True, group="dbgd", tag="dbgdst")
            op("gpsimd", lambda e: e.dma_start(out=dbg_oed, in_=oed_d),
               deps=oed_stores, dma=True, group="dbgo2", tag="dbgoed")
        oes_last = oed_stores[-2:]
        fin_ops = []
        gf_rd = [[], []]
        for tile in range(NT):
            combs = []
            for slot in range(2):
                gb = gfin[slot]
                gg = op("gpsimd", lambda e, gb=gb, tile=tile, slot=slot: e.indirect_dma_start(
                    out=gb, out_offset=None, in_=oed_d,
                    in_offset=IndirectOffsetOnAxis(
                        ap=dst_i32[:, 2 * tile + slot:2 * tile + slot + 1], axis=0)),
                    deps=gf_rd[slot] + oes_last, dma=True, group=f"fg{slot}",
                    tag=f"fg{tile}_{slot}")
                cc = op("vector", lambda e, gb=gb, tile=tile, slot=slot: e.scalar_tensor_tensor(
                    out=x_sb[:, tile * 1024:(tile + 1) * 1024], in0=gb,
                    scalar=cwsel[:, 2 * tile + slot:2 * tile + slot + 1],
                    in1=x_sb[:, tile * 1024:(tile + 1) * 1024],
                    op0=ALU.mult, op1=ALU.add),
                    deps=[gg, cw_ready[tile]] + combs, tag=f"fc{tile}_{slot}")
                gf_rd[slot] = [cc]
                combs.append(cc)
            fin_ops.append((tile, combs[-1]))

        outr = out_d.rearrange("(n p) d -> p n d", p=128)
        st_ops = []
        for b2 in range(NB):
            deps_b = [d for (t, d) in fin_ops if b2 * TPB <= t < (b2 + 1) * TPB]
            st_ops.append(op("sync", lambda e, b2=b2: e.dma_start(
                out=outr[:, b2 * TPB:(b2 + 1) * TPB, :],
                in_=x_sb[:, b2 * TPB * 1024:(b2 + 1) * TPB * 1024].rearrange(
                    "p (n d) -> p n d", d=1024)),
                deps=deps_b, dma=True, group="store", tag=f"st{b2}"))
        op("sync", lambda e: e.nop(), deps=st_ops, tag="final_join")

        assert a16.peak <= 67760, (a16.peak, a32.peak)
        assert a32.peak <= 19200, a32.peak
        P.emit(st)
    return nc


CFG_FULL = dict(T=2048, H=16, E=16)


def prep_inputs(L0_attn_x, L0_attn_Wq, L0_attn_Wk, L0_attn_Wv, L0_attn_Wbeta,
                L0_attn_b_beta, L0_attn_Wout, L0_ffn_Wg, L0_ffn_bg, L0_ffn_Wgu,
                L0_ffn_bgu, L0_ffn_Wd, L0_ffn_bd, cfg):
    T, H, E = cfg["T"], cfg["H"], cfg["E"]
    DI = H * 64
    c = np.ascontiguousarray
    f = np.asarray
    Wq = f(L0_attn_Wq); Wk = f(L0_attn_Wk); Wv = f(L0_attn_Wv)
    Wb = f(L0_attn_Wbeta); Wout = f(L0_attn_Wout)
    Wg = f(L0_ffn_Wg); Wgu = f(L0_ffn_Wgu); Wd = f(L0_ffn_Wd)
    bgu = f(L0_ffn_bgu)
    w = {
        "wq": c(Wq[:, :DI].reshape(8, 128, DI).transpose(1, 0, 2)),
        "wk": c(Wk[:, :DI].reshape(8, 128, DI).transpose(1, 0, 2)),
        "wv": c(Wv[:, :DI].reshape(8, 128, DI).transpose(1, 0, 2)),
        "wout": c(Wout[:DI].reshape(DI // 128, 128, 1024).transpose(1, 0, 2)),
        "wbeta": c(Wb[:, :H].reshape(8, 128, H).transpose(1, 0, 2)),
        "bbeta": c(f(L0_attn_b_beta)[:H]),
        "wgt": c(Wg[:E].T.reshape(8, 128, E).transpose(1, 0, 2)),
        "bg": c(f(L0_ffn_bg)[:E]),
        "wgate": c(Wgu[:E, :, 0::2].reshape(E, 8, 128, 1024).transpose(0, 2, 1, 3)),
        "wup": c(Wgu[:E, :, 1::2].reshape(E, 8, 128, 1024).transpose(0, 2, 1, 3)),
        "wd": c(Wd[:E].reshape(E, 8, 128, 1024).transpose(0, 2, 1, 3)),
        "bgug": c(bgu[:E, 0::2].reshape(E, 8, 128).transpose(0, 2, 1)),
        "bguu": c(bgu[:E, 1::2].reshape(E, 8, 128).transpose(0, 2, 1)),
        "bd": c(f(L0_ffn_bd)[:E]),
    }
    xs = np.asarray(L0_attn_x)
    return w, xs


def kernel(**inputs):
    cfg = CFG_FULL
    w, xs = prep_inputs(cfg=cfg, **inputs)
    nc = build(cfg)
    in_maps = []
    for i in range(8):
        m = dict(w)
        m["x"] = np.ascontiguousarray(xs[i, :cfg["T"], :])
        in_maps.append(m)
    res = run_bass_kernel_spmd(nc, in_maps, core_ids=list(range(8)))
    return np.stack([np.asarray(res.results[i]["out"]) for i in range(8)], axis=0)
